# version 02 (final)
# baseline (speedup 1.0000x reference)
import numpy as np
import ml_dtypes
from contextlib import ExitStack

import concourse.bass as bass
import concourse.mybir as mybir
from concourse.bass_utils import run_bass_kernel_spmd

F32 = mybir.dt.float32
BF16 = mybir.dt.bfloat16
I32 = mybir.dt.int32
AF = mybir.ActivationFunctionType
ALU = mybir.AluOpType
AX = mybir.AxisListType
BF = ml_dtypes.bfloat16

BATCH, SEQ, DM = 2, 4096, 2048
NTOK = BATCH * SEQ
TPC = NTOK // 8
DIN, DS, DCV, DTR = 4096, 16, 4, 128
FF0H, FF1, INT2, NE, HOLO = 8192, 8192, 5632, 8, 64
RND = 1.5 * 2.0 ** 23
EPS = 1e-5
ISQ128 = 1.0 / np.sqrt(128.0)

SCAN_DBG = False
L2_DBG = False
L3_CUT = 9
ENGINES = ("sync", "tensor", "scalar", "vector", "gpsimd")


def ecopy(E, out, in_):
    if hasattr(E, "tensor_copy"):
        return E.tensor_copy(out=out, in_=in_)
    return E.copy(out=out, in_=in_)




class Prog:
    def __init__(self, name):
        self.nc = bass.Bass("TRN2", target_bir_lowering=False,
                            detect_race_conditions=False)
        self.name = name
        self.es = ExitStack()
        self.plan = {e: [] for e in ENGINES}
        self.sems = {}
        self.cnt = {}
        self.eng_seen = {e: {} for e in ENGINES}
        self.D = {}
        self.final_tokens = []
        self._ccols = {}

    def ccol(self, v):
        if v not in self._ccols:
            t = self.sbuf(f"cc_{len(self._ccols)}", [128, 1], F32)
            tok = self.op("vector", lambda E, t=t, v=v: E.memset(t[:, :], v),
                          sig="VE")
            self._ccols[v] = (t[:, 0:1], tok)
        return self._ccols[v]

    def din(self, name, shape, dt=F32):
        self.D[name] = self.nc.dram_tensor(name, list(shape), dt,
                                           kind="ExternalInput").ap()
        return self.D[name]

    def dout(self, name, shape, dt=F32):
        self.D[name] = self.nc.dram_tensor(name, list(shape), dt,
                                           kind="ExternalOutput").ap()
        return self.D[name]

    def dint(self, name, shape, dt=F32):
        self.D[name] = self.nc.dram_tensor(name, list(shape), dt,
                                           kind="Internal").ap()
        return self.D[name]

    def sbuf(self, name, shape, dt=F32):
        return self.es.enter_context(self.nc.sbuf_tensor(name, list(shape), dt))

    def psum(self, name, shape, dt=F32):
        return self.es.enter_context(self.nc.psum_tensor(name, list(shape), dt))

    def sem(self, name):
        if name not in self.sems:
            self.sems[name] = self.es.enter_context(self.nc.semaphore(name))
            self.cnt[name] = 0
        return self.sems[name]

    def wait(self, eng, tok):
        if tok is None:
            return
        sname, val = tok
        if val <= 0:
            return
        if val <= self.eng_seen[eng].get(sname, 0):
            return
        self.eng_seen[eng][sname] = val
        sem = self.sem(sname)
        self.plan[eng].append(lambda E, sem=sem, val=val: E.wait_ge(sem, val))

    def op(self, eng, fn, waits=(), sig=None):
        for t in waits:
            self.wait(eng, t)
        if sig is None:
            sig = "S_" + eng
        sem = self.sem(sig)
        self.cnt[sig] += 1
        tok = (sig, self.cnt[sig])
        self.plan[eng].append(lambda E, fn=fn, sem=sem: fn(E).then_inc(sem, 1))
        return tok

    def dma(self, eng, out, in_, waits=(), sig=None):
        for t in waits:
            self.wait(eng, t)
        if sig is None:
            try:
                sig = "D_" + out.tensor.name
            except AttributeError:
                sig = "D_" + eng
        sem = self.sem(sig)
        self.cnt[sig] += 16
        tok = (sig, self.cnt[sig])
        self.plan[eng].append(
            lambda E, out=out, in_=in_, sem=sem:
            E.dma_start(out=out, in_=in_).then_inc(sem, 16))
        return tok

    def finalize(self):
        nc = self.nc
        with self.es:
            block = self.es.enter_context(nc.Block())

            def mk(eng_name):
                def body(E):
                    for fn in self.plan[eng_name]:
                        fn(E)
                    if eng_name == "gpsimd":
                        for (sname, val) in self.final_tokens:
                            E.wait_ge(self.sems[sname], val)
                return body

            for e in ("sync", "tensor", "scalar", "vector"):
                if self.plan[e]:
                    getattr(block, e)(mk(e))
            block.gpsimd(mk("gpsimd"))
        return nc


class PsumPool:

    def __init__(self, P, n, shape, pref, dt=F32):
        self.tiles = [P.psum(f"{pref}{i}", list(shape), dt) for i in range(n)]
        self.busy = [None] * n
        self.i = 0

    def get(self):
        i = self.i
        self.i = (self.i + 1) % len(self.tiles)
        return i, self.tiles[i], self.busy[i]

    def done(self, i, tok):
        self.busy[i] = tok


def mm_tok(P, statT, KT, ntok, w_dram, N, pool, evac, pref,
           extra_mm=None, n0=0, nchunk=512, wpan=None, stat_tok=None,
           tsub_list=None, in_eng="sync", pre_wait=None):
    if wpan is None:
        wpan = P.sbuf(pref + "_wp", [128, 2 * KT * nchunk], BF16)
    panel_free = [pre_wait, pre_wait]
    tsubs = tsub_list if tsub_list is not None else range(ntok // 128)
    last = None
    for nci in range((N - n0) // nchunk):
        ncol = n0 + nci * nchunk
        buf = nci % 2
        dts = []
        for kt in range(KT):
            dts.append(P.dma(in_eng,
                             wpan[:, (buf * KT + kt) * nchunk:(buf * KT + kt + 1) * nchunk],
                             w_dram[kt * 128:(kt + 1) * 128, ncol:ncol + nchunk],
                             waits=[panel_free[buf]] if kt == 0 and panel_free[buf] else ()))
        ptok = dts[-1]
        for ts in tsubs:
            pi, ps, pbusy = pool.get()
            w = [ptok, pbusy] + ([stat_tok] if stat_tok else [])
            for kt in range(KT):
                tok_mm = P.op("tensor",
                              lambda E, kt=kt, ts=ts, buf=buf, ps=ps:
                              E.matmul(out=ps[:, 0:nchunk],
                                       lhsT=statT[:, kt * ntok + ts * 128:kt * ntok + ts * 128 + 128],
                                       rhs=wpan[:, (buf * KT + kt) * nchunk:(buf * KT + kt + 1) * nchunk],
                                       start=(kt == 0),
                                       stop=(kt == KT - 1 and extra_mm is None)),
                              waits=w if kt == 0 else (), sig="MM")
            if extra_mm is not None:
                tok_mm = extra_mm(ps, ts, ncol)
            etok = evac(ps, ts, ncol, [tok_mm])
            pool.done(pi, etok)
            last = etok
        panel_free[buf] = last
    return last


def mm_dmaj(P, w_dram, DOUT, KT, xnT, ntok, pool, evac, pref,
            dgrp=512, tchunk=512, xn_tok=None, in_eng="sync", d0=0, d1=None,
            tstride=None, toff=0, wpan=None, pre_wait=None):
    if d1 is None:
        d1 = DOUT
    if tstride is None:
        tstride = ntok
    if wpan is None:
        wpan = P.sbuf(pref + "_wp", [128, 2 * KT * dgrp], BF16)
    panel_free = [pre_wait, pre_wait]
    last = None
    for dgi in range((d1 - d0) // dgrp):
        dcol = d0 + dgi * dgrp
        buf = dgi % 2
        dts = []
        for kt in range(KT):
            dts.append(P.dma(in_eng,
                             wpan[:, (buf * KT + kt) * dgrp:(buf * KT + kt + 1) * dgrp],
                             w_dram[kt * 128:(kt + 1) * 128, dcol:dcol + dgrp],
                             waits=[panel_free[buf]] if kt == 0 and panel_free[buf] else ()))
        ptok = dts[-1]
        for ds in range(dgrp // 128):
            for tc in range(ntok // tchunk):
                pi, ps, pbusy = pool.get()
                w = [ptok, pbusy] + ([xn_tok] if xn_tok else [])
                for kt in range(KT):
                    tok_mm = P.op("tensor",
                                  lambda E, kt=kt, ds=ds, tc=tc, buf=buf, ps=ps:
                                  E.matmul(out=ps[:, 0:tchunk],
                                           lhsT=wpan[:, (buf * KT + kt) * dgrp + ds * 128:(buf * KT + kt) * dgrp + ds * 128 + 128],
                                           rhs=xnT[:, kt * tstride + toff + tc * tchunk:kt * tstride + toff + tc * tchunk + tchunk],
                                           start=(kt == 0), stop=(kt == KT - 1)),
                                  waits=w if kt == 0 else (), sig="MM")
                etok = evac(ps, dcol + ds * 128, tc * tchunk, [tok_mm])
                pool.done(pi, etok)
                last = etok
        panel_free[buf] = last
    return last


def transpose_to(P, src_sb, dst_sb, rtiles, ctiles, ident, tps, src_cols,
                 dst_cols, waits=(), busy0=None):
    raise NotImplementedError


def emit_tp(P, src_ap_fn, dst_ap_fn, rtiles, ctiles, ident, tps, waits=(),
            pref="tp"):
    evac_tok = getattr(P, "last_tp", None)
    first = True
    for ri in range(rtiles):
        for cj in range(ctiles):
            w = list(waits) if first else []
            if evac_tok is not None:
                w.append(evac_tok)
            tok_mm = P.op("tensor",
                          lambda E, ri=ri, cj=cj:
                          E.transpose(out=tps[:, :], in_=src_ap_fn(ri, cj),
                                      identity=ident[:, :]),
                          waits=w, sig="MM")
            evac_tok = P.op("scalar",
                            lambda E, ri=ri, cj=cj:
                            ecopy(E, out=dst_ap_fn(ri, cj), in_=tps[:, :]),
                            waits=[tok_mm], sig="SC")
            first = False
    P.last_tp = evac_tok
    return evac_tok


def ln_tile(P, x_ap, xn_ap, W, col, sq_row, waits=(), eng_v="vector"):
    t1 = P.op(eng_v, lambda E: E.reduce_sum(out=col[:, 0:1], in_=x_ap, axis=AX.X),
              waits=waits, sig="VE")
    t2 = P.op("scalar", lambda E: E.activation(out=sq_row[:, 0:W], in_=x_ap,
                                               func=AF.Square,
                                               accum_out=col[:, 1:2]),
              waits=waits, sig="SC")
    t3 = P.op(eng_v, lambda E: E.tensor_scalar(out=col[:, 0:1], in0=col[:, 0:1],
                                               scalar1=1.0 / W, scalar2=None,
                                               op0=ALU.mult),
              waits=[t1], sig="VE")
    t4 = P.op(eng_v, lambda E: E.scalar_tensor_tensor(
        out=col[:, 2:3], in0=col[:, 0:1], scalar=-1.0, in1=col[:, 0:1],
        op0=ALU.mult, op1=ALU.mult), waits=[t3], sig="VE")
    t5 = P.op(eng_v, lambda E: E.scalar_tensor_tensor(
        out=col[:, 2:3], in0=col[:, 1:2], scalar=1.0 / W, in1=col[:, 2:3],
        op0=ALU.mult, op1=ALU.add), waits=[t2, t4], sig="VE")
    ceps, tcep = P.ccol(EPS)
    t6a = P.op("scalar", lambda E: E.activation(out=col[:, 4:5], in_=col[:, 2:3],
                                                func=AF.Sqrt, bias=ceps),
               waits=[t5, tcep], sig="SC")
    t6 = P.op(eng_v, lambda E: E.reciprocal(out=col[:, 3:4], in_=col[:, 4:5]),
              waits=[t6a], sig="VE")
    t7 = P.op(eng_v, lambda E: E.tensor_scalar(out=xn_ap, in0=x_ap,
                                               scalar1=col[:, 0:1],
                                               scalar2=col[:, 3:4],
                                               op0=ALU.subtract, op1=ALU.mult),
              waits=[t6], sig="VE")
    return t7


class StagePool:
    def __init__(self, P, n, shape, pref, dt=F32):
        self.tiles = [P.sbuf(f"{pref}{i}", list(shape), dt) for i in range(n)]
        self.busy = [None] * n
        self.i = 0

    def get(self):
        i = self.i
        self.i = (self.i + 1) % len(self.tiles)
        return i, self.tiles[i], self.busy[i]

    def done(self, i, tok):
        self.busy[i] = tok


def host_prep(inp):
    f = lambda k: np.asarray(inp[k], np.float32)
    A = np.ascontiguousarray
    d = {}
    d["WxT"] = A(f("L0_attn_Wx").T)
    d["WzT"] = A(f("L0_attn_Wz").T)
    i1 = (np.arange(1024) % 64) * 16 + (np.arange(1024) // 64)
    d["WBT"] = A(f("L0_attn_WB").T[:, i1])
    i2 = (np.arange(4096) % 256) * 16 + (np.arange(4096) // 256)
    d["WCT"] = A(f("L0_attn_WC").T[:, i2])
    d["WdtdT"] = A(f("L0_attn_Wdt_down").T)
    d["WdtT"] = A(f("L0_attn_Wdt").T)
    d["WoutT"] = A(f("L0_attn_Wout").T)
    d["bout_r"] = A(f("L0_attn_bout")[None, :])
    for nm in ("q", "k", "v", "o"):
        d["W" + nm + "T"] = A(f(f"L1_attn_W{nm}").T)
    d["gateT"] = A(f("L1_ffn_w_gate").T)
    d["upT"] = A(f("L1_ffn_w_up").T)
    d["downT"] = A(f("L1_ffn_w_down").T)
    d["W2kT"] = A(f("L2_attn_Wk").T)
    d["W2vT"] = A(f("L2_attn_Wv").T)
    d["W2rT"] = A(f("L2_attn_Wr").T)
    for nm in ("q", "k", "v", "o"):
        d[f"bas{nm}T"] = A(f(f"L3_attn_basis_{nm}").T)
        d[f"amp{nm}T"] = A(f(f"L3_attn_amp_{nm}").T)
        d[f"ph{nm}T"] = A(f(f"L3_attn_phase_{nm}").T)
    d["Wr3"] = A(f("L3_ffn_Wr"))
    d["We3r"] = A(f("L3_ffn_We").reshape(NE * DM, DM))
    d["QGV"] = A(f("L0_ffn_w_gv").T)
    d["QWO"] = A(f("L0_ffn_w_out").T)
    d["QG2"] = A(f("L2_ffn_gate_w").T)
    d["QD2"] = A(f("L2_ffn_down_w").T)
    cm = lambda v: A(np.asarray(v, np.float32).reshape(-1, 128).T)
    d["bx_c"] = cm(f("L0_attn_bx"))
    d["bz_c"] = cm(f("L0_attn_bz"))
    d["bdt_c"] = cm(f("L0_attn_bdt"))
    d["wnorm"] = f("L0_ffn_w_norm")
    d["td"] = f("L2_attn_time_decay")
    d["gn2"] = f("L2_ffn_gate_norm_w")
    d["dn2"] = f("L2_ffn_down_norm_w")
    d["A_log"] = f("L0_attn_A_log")
    d["convw"] = A(f("L0_attn_conv_w")[:, 0, :])
    d["convb"] = f("L0_attn_conv_b")
    d["Dvec"] = f("L0_attn_D")
    d["x0"] = f("L0_attn_hidden_states")
    d["ident"] = np.eye(128, dtype=BF)
    d["ones_r"] = np.ones((1, 128), dtype=BF)
    ql, kl = np.arange(512)[None, :], np.arange(128)[:, None]
    d["masks"] = np.stack([(ql >= kl + 128 * r) for r in range(4)]).astype(BF)
    half = 64
    theta = 10000.0 ** (-np.arange(half) * 2.0 / 128.0)
    ang = np.arange(SEQ)[:, None] * theta[None, :]
    d["rcos"] = A(np.cos(ang).astype(np.float32))
    d["rsin"] = A(np.sin(ang).astype(np.float32))
    repb = np.zeros((16, 32, 128), dtype=BF)
    for sidx in range(16):
        for p in range(128):
            repb[sidx, 2 * sidx + p // 64, p] = 1
    d["REPB"] = np.ascontiguousarray(repb.reshape(16 * 32, 128))
    repc = np.zeros((16, 128, 128), dtype=BF)
    for sidx in range(16):
        for p in range(128):
            repc[sidx, sidx * 8 + (p // 16), p] = 1
    d["REPC"] = np.ascontiguousarray(repc.reshape(16 * 128, 128))
    d["be3row"] = np.ascontiguousarray(
        np.asarray(inp["L3_ffn_be"], np.float32).reshape(1, NE * DM))
    return d


CVT = [
    ("WxT", 2048, 1024), ("WzT", 2048, 4096), ("WBT", 2048, 1024),
    ("WCT", 2048, 4096), ("WdtdT", 2048, 128), ("WdtT", 128, 4096),
    ("WoutT", 4096, 2048), ("bout_r", 1, 2048),
    ("WqT", 2048, 2048), ("WkT", 2048, 2048), ("WvT", 2048, 2048),
    ("WoT", 2048, 2048), ("gateT", 2048, 8192), ("upT", 2048, 8192),
    ("downT", 8192, 2048), ("W2kT", 2048, 2048), ("W2vT", 2048, 2048),
    ("W2rT", 2048, 2048),
    ("basqT", 2048, 64), ("baskT", 2048, 64), ("basvT", 2048, 64),
    ("basoT", 2048, 64), ("Wr3", 2048, 8), ("We3r", NE * DM, 2048),
    ("be3row", 1, NE * DM),
]
QNT = [("QGV", 2048, 16384), ("QWO", 8192, 2048), ("QG2", 2048, 11264),
       ("QD2", 5632, 2048)]


def build_prepA():
    P = Prog("prepA")
    bi = [P.sbuf(f"bi{i}", [128, 8192], F32) for i in range(2)]
    bo = [P.sbuf(f"bo{i}", [128, 8192], BF16) for i in range(2)]
    bo_busy = [None, None]
    col = P.sbuf("col", [128, 8], F32)
    tmpc = P.sbuf("tmpc", [128, 1], F32)
    engs = ["vector", "gpsimd", "scalar"]
    ei = [0]
    idx = [0]

    def conv_one(dsrc, ddst, rows, cols):
        for r0 in range(0, rows, 128):
            rr = min(128, rows - r0)
            for c0 in range(0, cols, 8192):
                cc = min(8192, cols - c0)
                k = idx[0] % 2
                idx[0] += 1
                tin = P.dma("sync", bi[k][0:rr, 0:cc],
                            dsrc[r0:r0 + rr, c0:c0 + cc],
                            waits=[bo_busy[k]] if bo_busy[k] else ())
                e = engs[ei[0] % 3]
                ei[0] += 1
                tcp = P.op(e, lambda E, k=k, rr=rr, cc=cc:
                           ecopy(E, out=bo[k][0:rr, 0:cc],
                                         in_=bi[k][0:rr, 0:cc]),
                           waits=[tin], sig="CP")
                bo_busy[k] = P.dma("gpsimd", ddst[r0:r0 + rr, c0:c0 + cc],
                                   bo[k][0:rr, 0:cc], waits=[tcp])
        return bo_busy[0]

    for (nm, rows, cols) in CVT:
        rpc = rows // 8 if rows >= 1024 else rows
        src = P.din("cv_" + nm, [rpc, cols], F32)
        dst = P.dout("co_" + nm, [rpc, cols], BF16)
        conv_one(src, dst, rpc, cols)
    qacc = P.sbuf("qacc", [128, 4], F32)
    tz = P.op("vector", lambda E: E.memset(qacc[:, :], 0.0), sig="VE")
    last_q = tz
    for m, (nm, rows, cols) in enumerate(QNT):
        rpc = rows // 8
        src = P.din("q_" + nm, [rpc, cols], F32)
        for r0 in range(0, rpc, 128):
            rr = min(128, rpc - r0)
            k = idx[0] % 2
            idx[0] += 1
            tin = P.dma("sync", bi[k][0:rr, 0:cols] if cols <= 8192 else None,
                        src[r0:r0 + rr, 0:cols] if cols <= 8192 else None,
                        waits=[bo_busy[k]] if bo_busy[k] else ()) if cols <= 8192 else None
            if cols <= 8192:
                t1 = P.op("vector", lambda E, k=k, rr=rr, cols=cols:
                          E.tensor_reduce(out=tmpc[0:rr, 0:1],
                                          in_=bi[k][0:rr, 0:cols], axis=AX.X,
                                          op=ALU.add, apply_absolute_value=True),
                          waits=[tin, last_q], sig="VE")
                last_q = P.op("vector", lambda E, m=m, rr=rr:
                              E.tensor_tensor(out=qacc[0:rr, m:m + 1],
                                              in0=qacc[0:rr, m:m + 1],
                                              in1=tmpc[0:rr, 0:1], op=ALU.add),
                              waits=[t1], sig="VE")
                bo_busy[k] = last_q
            else:
                for c0 in range(0, cols, 8192):
                    cc = min(8192, cols - c0)
                    k = idx[0] % 2
                    idx[0] += 1
                    tin = P.dma("sync", bi[k][0:rr, 0:cc],
                                src[r0:r0 + rr, c0:c0 + cc],
                                waits=[bo_busy[k]] if bo_busy[k] else ())
                    t1 = P.op("vector", lambda E, k=k, rr=rr, cc=cc:
                              E.tensor_reduce(out=tmpc[0:rr, 0:1],
                                              in_=bi[k][0:rr, 0:cc], axis=AX.X,
                                              op=ALU.add, apply_absolute_value=True),
                              waits=[tin, last_q], sig="VE")
                    last_q = P.op("vector", lambda E, m=m, rr=rr:
                                  E.tensor_tensor(out=qacc[0:rr, m:m + 1],
                                                  in0=qacc[0:rr, m:m + 1],
                                                  in1=tmpc[0:rr, 0:1], op=ALU.add),
                                  waits=[t1], sig="VE")
                    bo_busy[k] = last_q
    qout = P.dout("qsums", [128, 4], F32)
    tq = P.dma("gpsimd", qout, qacc[:, :], waits=[last_q])
    P.final_tokens.append(tq)
    for b in bo_busy:
        if b is not None:
            P.final_tokens.append(b)
    return P


def build_prepB():
    P = Prog("prepB")
    qs = P.din("qsums_all", [128, 32], F32)
    sq = P.sbuf("sq", [128, 32], F32)
    t4 = P.sbuf("t4", [4, 128], F32)
    red = P.sbuf("red", [128, 4], F32)
    scl = P.sbuf("scl", [128, 8], F32)
    mean4 = P.sbuf("mean4", [4, 1], F32)
    ones_f = P.sbuf("ones_f", [1, 128], F32)
    pcol = P.psum("pcol", [128, 1], F32)
    tin = P.dma("sync", sq[:, :], qs)
    t1 = P.op("vector", lambda E: E.tensor_reduce(
        out=red[:, 0:4], in_=sq[:, :].rearrange("p (m c) -> p m c", m=4),
        axis=AX.X, op=ALU.add), waits=[tin], sig="VE")
    tts = []
    for m in range(4):
        tts.append(P.dma("gpsimd", t4[m:m + 1, 0:128], red[0:128, m:m + 1],
                         waits=[t1]))
    t2 = P.op("vector", lambda E: E.tensor_reduce(
        out=t4[0:4, 0:1], in_=t4[0:4, 0:128], axis=AX.X, op=ALU.add),
        waits=tts, sig="VE")
    tone = P.op("vector", lambda E: E.memset(ones_f[:, :], 1.0), sig="VE")
    qinv = P.din("qinv", [4, 1])
    sqi = P.sbuf("sqi", [4, 1], F32)
    tqi = P.dma("sync", sqi[0:4, 0:1], qinv)
    tm = P.op("vector", lambda E: E.tensor_scalar(
        out=t4[0:4, 0:1], in0=t4[0:4, 0:1], scalar1=sqi[0:4, 0:1],
        scalar2=1e-5, op0=ALU.mult, op1=ALU.max), waits=[t2, tqi], sig="VE")
    tmean = P.op("vector", lambda E: ecopy(E, out=mean4[0:4, 0:1],
                                                   in_=t4[0:4, 0:1]),
                 waits=[tm], sig="VE")
    tinv = P.op("vector", lambda E: E.reciprocal(out=t4[0:4, 1:2],
                                                 in_=t4[0:4, 0:1]),
                waits=[tmean], sig="VE")
    invr = P.sbuf("invr", [1, 4], F32)
    tmv = []
    for m in range(4):
        tmv.append(P.dma("gpsimd", invr[0:1, m:m + 1], t4[m:m + 1, 1:2],
                         waits=[tinv]))
    stoks = []
    for m in range(4):
        tmm = P.op("tensor", lambda E, m=m: E.matmul(
            out=pcol[:, 0:1], lhsT=ones_f[0:1, 0:128], rhs=invr[0:1, m:m + 1],
            start=True, stop=True), waits=tmv + stoks[-1:], sig="MM")
        stoks.append(P.op("scalar", lambda E, m=m: ecopy(E, 
            out=scl[:, m:m + 1], in_=pcol[:, 0:1]), waits=[tmm], sig="SC"))
    mout = P.dout("wmeans", [4, 1], F32)
    tmo = P.dma("gpsimd", mout, mean4[0:4, 0:1], waits=[tmean])
    P.final_tokens.append(tmo)
    bi = [P.sbuf(f"bi{i}", [128, 8192], F32) for i in range(2)]
    bo = [P.sbuf(f"bo{i}", [128, 8192], BF16) for i in range(2)]
    bo_busy = [None, None]
    idx = [0]
    for m, (nm, rows, cols) in enumerate(QNT):
        rpc = rows // 8
        src = P.din("q_" + nm, [rpc, cols], F32)
        dst = P.dout("qo_" + nm, [rpc, cols], BF16)
        for r0 in range(0, rpc, 128):
            rr = min(128, rpc - r0)
            for c0 in range(0, cols, 8192):
                cc = min(8192, cols - c0)
                k = idx[0] % 2
                idx[0] += 1
                tin = P.dma("sync", bi[k][0:rr, 0:cc],
                            src[r0:r0 + rr, c0:c0 + cc],
                            waits=([bo_busy[k]] if bo_busy[k] else []) + [stoks[-1]])
                tq1 = P.op("vector", lambda E, k=k, rr=rr, cc=cc, m=m:
                           E.tensor_scalar(out=bi[k][0:rr, 0:cc],
                                           in0=bi[k][0:rr, 0:cc],
                                           scalar1=scl[0:rr, m:m + 1],
                                           scalar2=None, op0=ALU.mult),
                           waits=[tin], sig="VE")
                tq2 = P.op("vector", lambda E, k=k, rr=rr, cc=cc:
                           E.tensor_scalar(out=bi[k][0:rr, 0:cc],
                                           in0=bi[k][0:rr, 0:cc], scalar1=RND,
                                           scalar2=-RND, op0=ALU.add,
                                           op1=ALU.add), waits=[tq1], sig="VE")
                tq3 = P.op("vector", lambda E, k=k, rr=rr, cc=cc:
                           E.tensor_scalar(out=bo[k][0:rr, 0:cc],
                                           in0=bi[k][0:rr, 0:cc], scalar1=1.0,
                                           scalar2=-1.0, op0=ALU.min,
                                           op1=ALU.max), waits=[tq2], sig="VE")
                bo_busy[k] = P.dma("gpsimd", dst[r0:r0 + rr, c0:c0 + cc],
                                   bo[k][0:rr, 0:cc], waits=[tq3])
    for b in bo_busy:
        if b is not None:
            P.final_tokens.append(b)
    return P


def make_xnT(P, x_dram, nt, pref="xn", W=DM, ln=True):
    KT = W // 128
    ntok = nt * 128
    xt = P.sbuf(pref + "_xt", [128, W], F32)
    xnb = P.sbuf(pref + "_xnb", [128, W], BF16)
    col = P.sbuf(pref + "_col", [128, 8], F32)
    sq = P.sbuf(pref + "_sq", [128, W], F32)
    xnT = P.sbuf(pref + "T", [128, KT * ntok], BF16)
    ident = P.ident
    tps = P.tps
    tok = None
    for ts in range(nt):
        tin = P.dma("sync", xt[:, :], x_dram[ts * 128:(ts + 1) * 128, 0:W],
                    waits=[tok] if tok else ())
        if ln:
            tln = ln_tile(P, xt[:, :], xnb[:, :], W, col, sq, waits=[tin])
        else:
            tln = P.op("vector", lambda E: ecopy(E, out=xnb[:, :],
                                                         in_=xt[:, :]),
                       waits=[tin], sig="VE")
        tok = emit_tp(P,
                      lambda ri, cj, ts=ts: xnb[:, cj * 128:(cj + 1) * 128],
                      lambda ri, cj, ts=ts: xnT[:, cj * ntok + ts * 128:cj * ntok + ts * 128 + 128],
                      1, KT, ident, tps, waits=[tln])
    return xnT, tok


def build_l0tok():
    P = Prog("l0tok")
    x = P.din("x", [TPC, DM])
    WxT = P.din("WxT", [2048, 1024], BF16)
    WzT = P.din("WzT", [2048, 4096], BF16)
    WBT = P.din("WBT", [2048, 1024], BF16)
    WCT = P.din("WCT", [2048, 4096], BF16)
    WdtdT = P.din("WdtdT", [2048, 128], BF16)
    WdtT = P.din("WdtT", [128, 4096], BF16)
    bx_c = P.din("bx_c", [128, 8])
    bz_c = P.din("bz_c", [128, 32])
    bdt_c = P.din("bdt_c", [128, 32])
    ident = P.din("ident", [128, 128], BF16)
    XPJ = P.dout("XPJ", [1024, TPC])
    DLT = P.dout("DLT", [4096, TPC])
    BTS = P.dout("BTS", [1024, TPC])
    CTS = P.dout("CTS", [4096, TPC])
    SZT = P.dout("SZT", [4096, TPC])

    P.ident = P.sbuf("identsb", [128, 128], BF16)
    P.tps = P.psum("tps", [128, 128], BF16)
    ti = P.dma("sync", P.ident[:, :], ident)
    bxc = P.sbuf("bxc", [128, 8], F32)
    bzc = P.sbuf("bzc", [128, 32], F32)
    bdc = P.sbuf("bdc", [128, 32], F32)
    tb1 = P.dma("sync", bxc[:, :], bx_c)
    tb2 = P.dma("sync", bzc[:, :], bz_c)
    tb3 = P.dma("sync", bdc[:, :], bdt_c)
    tb4 = P.op("vector", lambda E: E.tensor_scalar(out=bdc[:, :], in0=bdc[:, :],
                                                   scalar1=2.0, scalar2=None,
                                                   op0=ALU.mult),
               waits=[tb3], sig="VE")
    P.wait("tensor", ti)
    xnT, txn = make_xnT(P, x, TPC // 128)

    pool = PsumPool(P, 4, [128, 512], "pp")
    stg = StagePool(P, 4, [128, 512], "st")
    w1 = P.sbuf("w1", [128, 512], F32)
    w2 = P.sbuf("w2", [128, 512], F32)

    def mk_evac(out_dram, act, bias_col, sig_extra=None):
        def evac(ps, dglob, tcol, waits):
            si, sb, sbusy = stg.get()
            w = list(waits) + ([sbusy] if sbusy else [])
            if bias_col is None:
                te = P.op("scalar", lambda E, ps=ps, sb=sb: E.activation(
                    out=sb[:, :], in_=ps[:, 0:512], func=act), waits=w, sig="SC")
            else:
                dt = dglob // 128
                act2 = AF.Identity if act == AF.Copy else act
                te = P.op("scalar", lambda E, ps=ps, sb=sb, dt=dt, act2=act2:
                          E.activation(out=sb[:, :], in_=ps[:, 0:512], func=act2,
                                       bias=bias_col[:, dt:dt + 1]),
                          waits=w, sig="SC")
            td = P.dma("gpsimd", out_dram[dglob:dglob + 128, tcol:tcol + 512],
                       sb[:, :], waits=[te])
            stg.done(si, td)
            return td
        return evac

    tz = mm_dmaj(P, WzT, 4096, 16, xnT, TPC, pool,
                 mk_evac(SZT, AF.Silu, bzc), "mz", xn_tok=txn)
    tc = mm_dmaj(P, WCT, 4096, 16, xnT, TPC, pool,
                 mk_evac(CTS, AF.Copy, None), "mc", xn_tok=txn)
    tb = mm_dmaj(P, WBT, 1024, 16, xnT, TPC, pool,
                 mk_evac(BTS, AF.Copy, None), "mb", xn_tok=txn)
    tx = mm_dmaj(P, WxT, 1024, 16, xnT, TPC, pool,
                 mk_evac(XPJ, AF.Copy, bxc), "mx", xn_tok=txn)
    d1T = P.sbuf("d1T", [128, 1024], BF16)

    def evac_d1(ps, dglob, tcol, waits):
        return P.op("scalar", lambda E, ps=ps, tcol=tcol: ecopy(E, 
            out=d1T[:, tcol:tcol + 512], in_=ps[:, 0:512]), waits=waits, sig="SC")

    td1 = mm_dmaj(P, WdtdT, 128, 16, xnT, TPC, pool, evac_d1, "md1",
                  dgrp=128, xn_tok=txn)

    prev_t5 = [None]

    def evac_dt(ps, dglob, tcol, waits):
        dt = dglob // 128
        si, sb, sbusy = stg.get()
        w = list(waits) + ([sbusy] if sbusy else []) + [tb4]
        if prev_t5[0] is not None:
            w.append(prev_t5[0])
        t1 = P.op("scalar", lambda E, ps=ps, dt=dt: E.activation(
            out=w1[:, :], in_=ps[:, 0:512], func=AF.Abs,
            bias=bdc[:, dt:dt + 1]), waits=w, sig="SC")
        t2 = P.op("scalar", lambda E: E.activation(
            out=w1[:, :], in_=w1[:, :], func=AF.Exp, scale=-1.0),
            waits=[t1], sig="SC")
        cone, tcon = P.ccol(1.0)
        t3 = P.op("scalar", lambda E: E.activation(
            out=w1[:, :], in_=w1[:, :], func=AF.Ln, bias=cone),
            waits=[t2, tcon], sig="SC")
        t4 = P.op("scalar", lambda E, ps=ps, dt=dt: E.activation(
            out=w2[:, :], in_=ps[:, 0:512], func=AF.Relu,
            bias=bdc[:, dt:dt + 1]), waits=[t3], sig="SC")
        t5 = P.op("vector", lambda E, sb=sb: E.tensor_tensor(
            out=sb[:, :], in0=w1[:, :], in1=w2[:, :], op=ALU.add),
            waits=[t4], sig="VE")
        prev_t5[0] = t5
        td = P.dma("gpsimd", DLT[dglob:dglob + 128, tcol:tcol + 512],
                   sb[:, :], waits=[t5])
        stg.done(si, td)
        return td

    tdt = mm_dmaj(P, WdtT, 4096, 1, d1T, TPC, pool, evac_dt, "mdt",
                  xn_tok=td1)
    P.final_tokens += [tz, tc, tb, tx, tdt]
    return P


def build_l0scan():
    P = Prog("l0scan")
    xpj = P.din("xpj", [256, SEQ])
    dlt = P.din("dlt", [1024, SEQ])
    bts = P.din("bts", [256, SEQ])
    cts = P.din("cts", [1024, SEQ])
    szt = P.din("szt", [1024, SEQ])
    alog = P.din("alog", [1024, 16])
    cw = P.din("convw", [1024, 4])
    cb_c = P.din("convb_c", [128, 8])
    dv_c = P.din("dvec_c", [128, 8])
    repb = P.din("REPB", [512, 128], BF16)
    repc = P.din("REPC", [2048, 128], BF16)
    YFT = P.dout("YFT", [1024, SEQ], BF16)
    if SCAN_DBG:
        DBG_XP = P.dout("DBG_XP", [128, SEQ])
        DBG_U = P.dout("DBG_U", [128, SEQ])
        DBG_DU = P.dout("DBG_DU", [128, SEQ])
        DBG_DA = P.dout("DBG_DA", [128, SEQ])
        DBG_DBU = P.dout("DBG_DBU", [128, SEQ])
        DBG_H = P.dout("DBG_H", [128, SEQ])
        DBG_Y = P.dout("DBG_Y", [128, SEQ])

    srepb = P.sbuf("srepb", [32, 16 * 128], BF16)
    srepc = P.sbuf("srepc", [128, 16 * 128], BF16)
    scb = P.sbuf("scb", [128, 8], F32)
    sdv = P.sbuf("sdv", [128, 8], F32)
    pre = [P.dma("sync", scb[:, :], cb_c), P.dma("sync", sdv[:, :], dv_c)]
    for sidx in range(16):
        pre.append(P.dma("sync", srepb[:, sidx * 128:(sidx + 1) * 128],
                         repb[sidx * 32:(sidx + 1) * 32, :]))
        pre.append(P.dma("sync", srepc[:, sidx * 128:(sidx + 1) * 128],
                         repc[sidx * 128:(sidx + 1) * 128, :]))
    asb = P.sbuf("asb", [128, 8 * 16], F32)
    ta = [P.dma("sync", asb[:, ct * 16:(ct + 1) * 16],
                alog[ct * 128:(ct + 1) * 128, 0:16]) for ct in range(8)]
    t2 = P.op("scalar", lambda E: E.activation(out=asb[:, :], in_=asb[:, :],
                                               func=AF.Exp), waits=ta, sig="SC")
    tA = P.op("vector", lambda E: E.tensor_scalar(
        out=asb[:, :], in0=asb[:, :], scalar1=-1.0, scalar2=None,
        op0=ALU.mult), waits=[t2], sig="VE")
    cwc = P.sbuf("cwc", [128, 8 * 4], F32)
    tcw = [P.dma("sync", cwc[:, ct * 4:(ct + 1) * 4],
                 cw[ct * 128:(ct + 1) * 128, 0:4]) for ct in range(8)]

    xps = P.sbuf("xps", [128, SEQ + 3 + 1], F32)
    usb = P.sbuf("usb", [128, SEQ], F32)
    dsb = P.sbuf("dsb", [128, SEQ], F32)
    dusb = P.sbuf("dusb", [128, SEQ], F32)
    dA = P.sbuf("dA", [128, SEQ], F32)
    hsb = P.sbuf("hsb", [128, SEQ], F32)
    yacc = P.sbuf("yacc", [128, SEQ], F32)
    szs = P.sbuf("szs", [128, SEQ], F32)
    btmp = P.sbuf("btmp", [32, SEQ], F32)
    bbf = P.sbuf("bbf", [32, SEQ], BF16)
    cbf = P.sbuf("cbf", [128, SEQ], BF16)
    ybf = P.sbuf("ybf", [128, SEQ], BF16)
    dbu = P.sbuf("dbu", [128, SEQ], F32)
    prod = P.sbuf("prod", [128, 512], F32)
    psB = P.psum("psB", [128, 512], F32)
    psC = P.psum("psC", [128, 512], F32)
    NCK = SEQ // 512

    prev_out = None
    for ct in range(8):
        w0 = [tA] + tcw + pre
        if prev_out is not None:
            w0 = w0 + [prev_out]
        txp = None
        for xg in range(2):
            txp = P.dma("sync", xps[64 * xg:64 * (xg + 1), 3:3 + SEQ],
                        xpj[32 * ct + 16 * xg:32 * ct + 16 * (xg + 1), :]
                        .rearrange("(o i) t -> o i t", o=1)
                        .broadcast_to([4, 16, SEQ]), waits=w0)
        tbt = P.dma("sync", btmp[:, :], bts[ct * 32:(ct + 1) * 32, :],
                    waits=w0)
        tbb = P.op("vector", lambda E: ecopy(E, out=bbf[:, :], in_=btmp[:, :]),
                   waits=[tbt], sig="VE")
        tmz = P.op("vector", lambda E: E.memset(xps[:, 0:3], 0.0),
                   waits=w0, sig="VE")
        tu = P.op("vector", lambda E, ct=ct: E.tensor_scalar(
            out=usb[:, :], in0=xps[:, 0:SEQ], scalar1=cwc[:, ct * 4:ct * 4 + 1],
            scalar2=None, op0=ALU.mult), waits=[txp, tmz], sig="VE")
        for k in range(1, 4):
            tu = P.op("vector", lambda E, k=k, ct=ct: E.scalar_tensor_tensor(
                out=usb[:, :], in0=xps[:, k:k + SEQ],
                scalar=cwc[:, ct * 4 + k:ct * 4 + k + 1], in1=usb[:, :],
                op0=ALU.mult, op1=ALU.add), waits=[tu], sig="VE")
        tu = P.op("scalar", lambda E, ct=ct: E.activation(
            out=usb[:, :], in_=usb[:, :], func=AF.Silu,
            bias=scb[:, ct:ct + 1]), waits=[tu], sig="SC")
        if SCAN_DBG and ct == 0:
            tu = P.dma("gpsimd", P.D["DBG_U"], usb[:, :], waits=[tu])
            tu = P.dma("gpsimd", P.D["DBG_XP"], xps[:, 3:3 + SEQ], waits=[tu])
        tdl = P.dma("sync", dsb[:, :], dlt[ct * 128:(ct + 1) * 128, :],
                    waits=[prev_out] if prev_out else ())
        tdu = P.op("vector", lambda E: E.tensor_tensor(
            out=dusb[:, :], in0=dsb[:, :], in1=usb[:, :], op=ALU.mult),
            waits=[tdl, tu], sig="VE")
        tcl = P.dma("sync", xps[:, 4:4 + SEQ],
                    cts[:, :].rearrange("(s h) t -> s h t", s=16)[:, 8 * ct:8 * ct + 8, :],
                    waits=[tu])
        tcb = P.op("vector", lambda E: ecopy(E, out=cbf[:, :],
                                                     in_=xps[:, 4:4 + SEQ]),
                   waits=[tcl], sig="VE")
        tya = P.op("vector", lambda E: E.memset(yacc[:, :], 0.0),
                   waits=[prev_out] if prev_out else (), sig="VE")
        tsz = P.dma("sync", szs[:, :], szt[ct * 128:(ct + 1) * 128, :],
                    waits=[prev_out] if prev_out else ())
        yacc_tok = tya
        pB_free = None
        pC_free = None
        for s in range(16):
            tda = P.op("scalar", lambda E, s=s, ct=ct: E.activation(
                out=dA[:, :], in_=dsb[:, :], func=AF.Exp,
                scale=asb[:, ct * 16 + s:ct * 16 + s + 1]),
                waits=[tdl, tA, yacc_tok if s > 0 else tya], sig="SC")
            tdbu = None
            for ck in range(NCK):
                c0 = ck * 512
                wB = [pB_free] if pB_free else []
                tmmB = P.op("tensor", lambda E, s=s, c0=c0: E.matmul(
                    out=psB[:, 0:512],
                    lhsT=srepb[:, s * 128:(s + 1) * 128],
                    rhs=bbf[0:32, c0:c0 + 512], start=True, stop=True,
                    skip_group_check=True),
                    waits=wB + [tbb], sig="MM")
                tdbu = P.op("vector", lambda E, c0=c0: E.tensor_tensor(
                    out=dbu[:, c0:c0 + 512], in0=psB[:, 0:512],
                    in1=dusb[:, c0:c0 + 512],
                    op=ALU.mult), waits=[tmmB, tdu], sig="VE")
                pB_free = tdbu
            tsc = P.op("vector", lambda E: E.tensor_tensor_scan(
                out=hsb[:, :], data0=dA[:, :], data1=dbu[:, :], initial=0.0,
                op0=ALU.mult, op1=ALU.add), waits=[tdbu, tda], sig="VE")
            for ck in range(NCK):
                c0 = ck * 512
                wC = [pC_free] if pC_free else []
                tmmC = P.op("tensor", lambda E, s=s, c0=c0: E.matmul(
                    out=psC[:, 0:512],
                    lhsT=srepc[:, s * 128:(s + 1) * 128],
                    rhs=cbf[:, c0:c0 + 512],
                    start=True, stop=True, skip_group_check=True),
                    waits=wC + [tcb], sig="MM")
                tpr = P.op("vector", lambda E, c0=c0: E.tensor_tensor(
                    out=prod[:, :], in0=psC[:, 0:512], in1=hsb[:, c0:c0 + 512],
                    op=ALU.mult), waits=[tmmC, tsc, yacc_tok], sig="VE")
                pC_free = tpr
                yacc_tok = P.op("gpsimd", lambda E, c0=c0: E.tensor_tensor(
                    out=yacc[:, c0:c0 + 512], in0=yacc[:, c0:c0 + 512],
                    in1=prod[:, :], op=ALU.add), waits=[tpr, yacc_tok], sig="GP")
        if SCAN_DBG and ct == 0:
            for nm_, src_ in [("DBG_DU", dusb), ("DBG_DA", dA),
                              ("DBG_DBU", dbu), ("DBG_H", hsb), ("DBG_Y", yacc)]:
                P.final_tokens.append(
                    P.dma("gpsimd", P.D[nm_], src_[:, 0:SEQ],
                          waits=[yacc_tok]))
        tf1 = P.op("vector", lambda E, ct=ct: E.scalar_tensor_tensor(
            out=usb[:, :], in0=usb[:, :], scalar=sdv[:, ct:ct + 1],
            in1=yacc[:, :], op0=ALU.mult, op1=ALU.add),
            waits=[yacc_tok, tsz], sig="VE")
        tf2 = P.op("vector", lambda E: E.tensor_tensor(
            out=ybf[:, :], in0=usb[:, :], in1=szs[:, :], op=ALU.mult),
            waits=[tf1], sig="VE")
        prev_out = P.dma("gpsimd", YFT[ct * 128:(ct + 1) * 128, :],
                         ybf[:, :], waits=[tf2])
        P.final_tokens.append(prev_out)
    return P


def bcast_row(P, dst_sb, src_dram, W, waits=()):
    return P.dma("sync", dst_sb,
                 src_dram.rearrange("(o w) -> o w", o=1).broadcast_to([128, W]),
                 waits=waits)


def quant_cols(P, src_ap, W, col, base, eng="vector", waits=()):
    t1 = P.op(eng, lambda E: E.tensor_reduce(
        out=col[:, base:base + 1], in_=src_ap, axis=AX.X, op=ALU.max,
        apply_absolute_value=True), waits=waits, sig="VE")
    t2 = P.op(eng, lambda E: E.tensor_scalar(
        out=col[:, base:base + 1], in0=col[:, base:base + 1], scalar1=1e-5,
        scalar2=None, op0=ALU.max), waits=[t1], sig="VE")
    t3 = P.op(eng, lambda E: E.reciprocal(
        out=col[:, base + 1:base + 2], in_=col[:, base:base + 1]),
        waits=[t2], sig="VE")
    t4 = P.op(eng, lambda E: E.tensor_scalar(
        out=col[:, base + 1:base + 2], in0=col[:, base + 1:base + 2],
        scalar1=127.0, scalar2=None, op0=ALU.mult), waits=[t3], sig="VE")
    return t4


def round_clip(P, dst_ap, src_ap, asc_col, lo, hi, eng="vector", waits=()):
    t1 = P.op(eng, lambda E: E.tensor_scalar(
        out=dst_ap, in0=src_ap, scalar1=asc_col, scalar2=None, op0=ALU.mult),
        waits=waits, sig="VE")
    t2 = P.op(eng, lambda E: E.tensor_scalar(
        out=dst_ap, in0=dst_ap, scalar1=RND, scalar2=-RND, op0=ALU.add,
        op1=ALU.add), waits=[t1], sig="VE")
    t3 = P.op(eng, lambda E: E.tensor_scalar(
        out=dst_ap, in0=dst_ap, scalar1=float(hi), scalar2=float(lo),
        op0=ALU.min, op1=ALU.max), waits=[t2], sig="VE")
    return t3


def build_l0ffn():
    P = Prog("l0ffn")
    x = P.din("x", [TPC, DM])
    yft = P.din("yft", [DIN, TPC], BF16)
    WoutT = P.din("WoutT", [DIN, DM], BF16)
    bout_r = P.din("bout_r", [1, DM], BF16)
    wnorm = P.din("wnorm", [DM])
    QGV = P.din("QGVb", [DM, 2 * FF0H], BF16)
    QWO = P.din("QWOb", [FF0H, DM], BF16)
    wmeans = P.din("wmeans", [4, 1])
    ident = P.din("ident", [128, 128], BF16)
    ones_r = P.din("ones_r", [1, 128], BF16)
    X1 = P.dout("X1", [TPC, DM])
    U2 = P.dint("U2", [TPC, FF0H], BF16)
    G0D = P.dint("G0D", [TPC, DM], F32)

    P.ident = P.sbuf("identsb", [128, 128], BF16)
    P.tps = P.psum("tps", [128, 128], BF16)
    ti = P.dma("sync", P.ident[:, :], ident)
    P.wait("tensor", ti)
    sones = P.sbuf("sones", [1, 128], BF16)
    tio = P.dma("sync", sones[:, :], ones_r)
    sbout = P.sbuf("sbout", [1, DM], BF16)
    tbo = P.dma("sync", sbout[:, :], bout_r)
    col = P.sbuf("col", [128, 16], F32)
    c1s = P.sbuf("c1s", [128, 8], F32)
    c2s = P.sbuf("c2s", [128, 8], F32)
    m2s = P.sbuf("m2s", [128, 8], F32)
    mgv = P.sbuf("mgv", [128, 2], F32)
    tm1 = P.dma("sync", mgv[:, 0:1],
                wmeans[0:1, 0:1].broadcast_to([128, 1]))
    tm2 = P.dma("sync", mgv[:, 1:2],
                wmeans[1:2, 0:1].broadcast_to([128, 1]))
    ARB = P.sbuf("ARB", [128, 32768], BF16)
    ARF = P.sbuf("ARF", [128, 10240], F32)
    WRK = P.sbuf("WRK", [128, 6144], F32)
    WPO = P.sbuf("WPO", [128, 16384], BF16)
    wpv = WPO[:, 0:8192]
    wpg = WPO[:, 8192:16384]
    xqb = P.sbuf("xqb", [128, DM], BF16)
    u2st = StagePool(P, 2, [128, 512], "u2s", BF16)
    xst = StagePool(P, 2, [128, 512], "xs", F32)
    x1st = StagePool(P, 2, [128, 512], "x1s", F32)
    gst = StagePool(P, 2, [128, 512], "gs", F32)
    g0st = StagePool(P, 2, [128, 256], "g0s", F32)
    pool = PsumPool(P, 4, [128, 512], "pp")

    tmz = P.op("vector", lambda E: E.memset(m2s[:, :], 0.0), sig="VE")
    tc1_l = [None] * 8
    wnb = ARF[:, 0:2048]
    twn = bcast_row(P, wnb, wnorm, DM)
    xt = ARF[:, 2048:4096]
    xnf = ARF[:, 4096:6144]
    sq = ARF[:, 6144:8192]
    hb = ARF[:, 8192:10240]
    xqT = ARB[:, 0:16384]
    tok = None
    for ts in range(8):
        tin = P.dma("sync", xt, x[ts * 128:(ts + 1) * 128, :],
                    waits=[tok] if tok else ())
        tln = ln_tile(P, xt, xnf, DM, col, sq, waits=[tin])
        ts1 = P.op("scalar", lambda E: E.activation(
            out=sq, in_=xnf, func=AF.Square, accum_out=col[:, 4:5]),
            waits=[tln], sig="SC")
        ceps, tcep = P.ccol(EPS)
        ts2a = P.op("scalar", lambda E: E.activation(
            out=col[:, 5:6], in_=col[:, 4:5], func=AF.Sqrt,
            scale=1.0 / DM, bias=ceps), waits=[ts1, tcep], sig="SC")
        ts2 = P.op("vector", lambda E: E.reciprocal(
            out=col[:, 5:6], in_=col[:, 5:6]), waits=[ts2a], sig="VE")
        ts3 = P.op("vector", lambda E: E.tensor_scalar(
            out=hb, in0=xnf, scalar1=col[:, 5:6], scalar2=None, op0=ALU.mult),
            waits=[ts2], sig="VE")
        ts4 = P.op("vector", lambda E: E.tensor_tensor(
            out=hb, in0=hb, in1=wnb, op=ALU.mult), waits=[ts3, twn], sig="VE")
        tq = quant_cols(P, hb, DM, col, 6, waits=[ts4])
        tc1_l[ts] = P.op("vector", lambda E, ts=ts: E.tensor_scalar(
            out=c1s[:, ts:ts + 1], in0=col[:, 6:7], scalar1=mgv[:, 0:1],
            scalar2=1.0 / 127.0, op0=ALU.mult, op1=ALU.mult),
            waits=[tq, tm1], sig="VE")
        trc = round_clip(P, xqb[:, :], hb, col[:, 7:8], -128, 127,
                         waits=[tq])
        tok = emit_tp(P, lambda ri, cj: xqb[:, cj * 128:(cj + 1) * 128],
                      lambda ri, cj, ts=ts: xqT[:, cj * 1024 + ts * 128:cj * 1024 + ts * 128 + 128],
                      1, 16, P.ident, P.tps, waits=[trc])
    txq = tok
    w1 = WRK[:, 0:512]
    w2 = WRK[:, 512:1024]
    panel_free = [None]
    for j in range(16):
        dtv = []
        for kt in range(16):
            dtv.append(P.dma("sync", wpv[:, kt * 512:(kt + 1) * 512],
                             QGV[kt * 128:(kt + 1) * 128, j * 512:(j + 1) * 512],
                             waits=[panel_free[0]] if kt == 0 and panel_free[0] else ()))
        for kt in range(16):
            dtv.append(P.dma("sync", wpg[:, kt * 512:(kt + 1) * 512],
                             QGV[kt * 128:(kt + 1) * 128, FF0H + j * 512:FF0H + (j + 1) * 512]))
        ptok = dtv[-1]
        last = None
        for ts in range(8):
            pi, psv, pbv = pool.get()
            pj, psg, pbg = pool.get()
            wv = [ptok, txq, pbv, pbg]
            for kt in range(16):
                P.op("tensor", lambda E, kt=kt, ts=ts, psv=psv: E.matmul(
                    out=psv[:, :], lhsT=xqT[:, kt * 1024 + ts * 128:kt * 1024 + ts * 128 + 128],
                    rhs=wpv[:, kt * 512:(kt + 1) * 512],
                    start=(kt == 0), stop=(kt == 15)),
                    waits=wv if kt == 0 else (), sig="MM")
            tmm2 = None
            for kt in range(16):
                tmm2 = P.op("tensor", lambda E, kt=kt, ts=ts, psg=psg: E.matmul(
                    out=psg[:, :], lhsT=xqT[:, kt * 1024 + ts * 128:kt * 1024 + ts * 128 + 128],
                    rhs=wpg[:, kt * 512:(kt + 1) * 512],
                    start=(kt == 0), stop=(kt == 15)), sig="MM")
            te1 = P.op("scalar", lambda E, ts=ts, psv=psv: E.activation(
                out=w1, in_=psv[:, :], func=AF.Copy, scale=c1s[:, ts:ts + 1]),
                waits=[tmm2, tc1_l[ts]], sig="SC")
            te2 = P.op("scalar", lambda E, ts=ts, psg=psg: E.activation(
                out=w2, in_=psg[:, :], func=AF.Silu, scale=c1s[:, ts:ts + 1]),
                waits=[tmm2], sig="SC")
            si, ub, ubusy = u2st.get()
            tu2 = P.op("vector", lambda E, ub=ub: E.tensor_tensor(
                out=ub[:, :], in0=w1, in1=w2, op=ALU.mult),
                waits=[te1, te2] + ([ubusy] if ubusy else []), sig="VE")
            pool.done(pi, tu2)
            pool.done(pj, tu2)
            tmx = P.op("vector", lambda E, ub=ub: E.tensor_reduce(
                out=col[:, 8:9], in_=ub[:, :], axis=AX.X, op=ALU.max,
                apply_absolute_value=True), waits=[tu2, tmz], sig="VE")
            P.op("vector", lambda E, ts=ts: E.tensor_tensor(
                out=m2s[:, ts:ts + 1], in0=m2s[:, ts:ts + 1], in1=col[:, 8:9],
                op=ALU.max), waits=[tmx], sig="VE")
            tdu = P.dma("gpsimd", U2[ts * 128:(ts + 1) * 128, j * 512:(j + 1) * 512],
                        ub[:, :], waits=[tu2])
            u2st.done(si, tdu)
            last = tdu
        panel_free[0] = last
    tB = panel_free[0]
    tasc = []
    for ts in range(8):
        t1 = P.op("vector", lambda E, ts=ts: E.tensor_scalar(
            out=m2s[:, ts:ts + 1], in0=m2s[:, ts:ts + 1], scalar1=1e-5,
            scalar2=None, op0=ALU.max), waits=[tB], sig="VE")
        t2 = P.op("vector", lambda E, ts=ts: E.tensor_scalar(
            out=c2s[:, ts:ts + 1], in0=m2s[:, ts:ts + 1], scalar1=mgv[:, 1:2],
            scalar2=1.0 / 127.0, op0=ALU.mult, op1=ALU.mult),
            waits=[t1, tm2], sig="VE")
        t3 = P.op("vector", lambda E, ts=ts: E.reciprocal(
            out=col[:, 9:10], in_=m2s[:, ts:ts + 1]), waits=[t2], sig="VE")
        t4 = P.op("vector", lambda E, ts=ts: E.tensor_scalar(
            out=m2s[:, ts:ts + 1], in0=col[:, 9:10], scalar1=127.0,
            scalar2=None, op0=ALU.mult), waits=[t3], sig="VE")
        tasc.append(t4)
    ubc = xqb
    q2f = WRK[:, 0:2048]
    q2b = P.sbuf("q2b", [128, 2048], BF16)
    for th in range(2):
        txq2 = None
        for tsl in range(4):
            ts = th * 4 + tsl
            for cc in range(4):
                tin = P.dma("sync", ubc[:, :],
                            U2[ts * 128:(ts + 1) * 128, cc * 2048:(cc + 1) * 2048],
                            waits=[txq2] if txq2 else [tB])
                trc = round_clip(P, q2f, ubc[:, :], m2s[:, ts:ts + 1],
                                 -128, 127, waits=[tin] + tasc)
                tcb = P.op("vector", lambda E: ecopy(E, 
                    out=q2b[:, :], in_=q2f),
                    waits=[trc] + ([txq2] if txq2 else []), sig="VE")
                txq2 = emit_tp(P, lambda ri, cj: q2b[:, cj * 128:(cj + 1) * 128],
                               lambda ri, cj, ts=ts, tsl=tsl, cc=cc:
                               ARB[:, (cc * 16 + cj) * 512 + tsl * 128:(cc * 16 + cj) * 512 + tsl * 128 + 128],
                               1, 16, P.ident, P.tps, waits=[tcb])
        pfree = None
        for dc in range(8):
            dts = []
            for kt in range(64):
                dts.append(P.dma("sync", WPO[:, kt * 256:(kt + 1) * 256],
                                 QWO[kt * 128:(kt + 1) * 128, dc * 256:(dc + 1) * 256],
                                 waits=[pfree] if kt == 0 and pfree else ()))
            ptok = dts[-1]
            for tsl in range(4):
                ts = th * 4 + tsl
                pi, ps, pb = pool.get()
                w = [ptok, txq2] + ([pb] if pb else [])
                tmm = None
                for kt in range(64):
                    tmm = P.op("tensor", lambda E, kt=kt, tsl=tsl, ps=ps: E.matmul(
                        out=ps[:, 0:256],
                        lhsT=ARB[:, kt * 512 + tsl * 128:kt * 512 + tsl * 128 + 128],
                        rhs=WPO[:, kt * 256:(kt + 1) * 256],
                        start=(kt == 0), stop=(kt == 63)),
                        waits=w if kt == 0 else (), sig="MM")
                si0, sg0, sg0b = g0st.get()
                te = P.op("scalar", lambda E, ts=ts, dc=dc, ps=ps, sg0=sg0:
                          E.activation(out=sg0[:, :], in_=ps[:, 0:256],
                                       func=AF.Copy, scale=c2s[:, ts:ts + 1]),
                          waits=[tmm] + ([sg0b] if sg0b else []), sig="SC")
                tdg = P.dma("gpsimd",
                            G0D[ts * 128:(ts + 1) * 128, dc * 256:(dc + 1) * 256],
                            sg0[:, :], waits=[te])
                g0st.done(si0, tdg)
                pool.done(pi, te)
                pfree = tdg
    tC = pfree
    xt2 = WRK[:, 0:2048]
    xn2 = WRK[:, 2048:4096]
    sq2 = WRK[:, 4096:6144]
    grow = ARF[:, 0:2048]
    tok = tC
    tglast = tC
    for ts in range(8):
        tin = P.dma("sync", xt2, x[ts * 128:(ts + 1) * 128, :], waits=[tok])
        tgr = P.dma("sync", grow, G0D[ts * 128:(ts + 1) * 128, :],
                    waits=[tok, tC])
        tln = ln_tile(P, xt2, xn2, DM, col, sq2, waits=[tin])
        ta = P.op("vector", lambda E: E.tensor_tensor(
            out=grow, in0=grow, in1=xn2, op=ALU.add),
            waits=[tln, tgr], sig="VE")
        tsl = P.op("scalar", lambda E: E.activation(
            out=grow, in_=grow, func=AF.Silu), waits=[ta], sig="SC")
        tok = P.dma("gpsimd", G0D[ts * 128:(ts + 1) * 128, :], grow,
                    waits=[tsl])
        tglast = tok
    yT = ARB
    tyl = [tglast]
    for kt in range(32):
        tyl.append(P.dma("sync", yT[:, kt * 1024:(kt + 1) * 1024],
                         yft[kt * 128:(kt + 1) * 128, :], waits=[txq2]))
    panel_tok = {}
    prevmm = [tglast]

    def s0_mm(ts, nc, ps, waits):
        if nc not in panel_tok:
            dts = []
            for kt in range(32):
                dts.append(P.dma("sync", WPO[:, kt * 512:(kt + 1) * 512],
                                 WoutT[kt * 128:(kt + 1) * 128, nc * 512:(nc + 1) * 512],
                                 waits=tyl + [prevmm[0]] if kt == 0 else ()))
            panel_tok[nc] = dts[-1]
        w = [panel_tok[nc]] + tyl + list(waits)
        for kt in range(32):
            P.op("tensor", lambda E, kt=kt, ts=ts, ps=ps: E.matmul(
                out=ps[:, :],
                lhsT=yT[:, kt * 1024 + ts * 128:kt * 1024 + ts * 128 + 128],
                rhs=WPO[:, kt * 512:(kt + 1) * 512],
                start=(kt == 0), stop=False), waits=w if kt == 0 else (),
                sig="MM")
        tmm = P.op("tensor", lambda E, nc=nc, ps=ps: E.matmul(
            out=ps[:, :], lhsT=sones[0:1, 0:128],
            rhs=sbout[0:1, nc * 512:(nc + 1) * 512],
            start=False, stop=True), waits=[tio, tbo], sig="MM")
        prevmm[0] = tmm
        return tmm

    lastx = emit_combine(P, x, X1, G0D, s0_mm, pool, xst, x1st, gst)
    P.final_tokens.append(lastx)
    return P


def _launch(P, in_maps):
    nc = P.finalize()
    return run_bass_kernel_spmd(nc, in_maps, core_ids=list(range(8)))


def cslice(c):
    b, q = c >> 2, c & 3
    return b * SEQ + q * 1024


def run_prep(prep):
    Pa = build_prepA()
    ims = []
    for c in range(8):
        m = {}
        for (nm, rows, cols) in CVT:
            arr = prep[nm]
            if rows >= 1024:
                rpc = rows // 8
                m["cv_" + nm] = np.ascontiguousarray(arr[c * rpc:(c + 1) * rpc])
            else:
                m["cv_" + nm] = arr
        for (nm, rows, cols) in QNT:
            rpc = rows // 8
            m["q_" + nm] = np.ascontiguousarray(prep[nm][c * rpc:(c + 1) * rpc])
        ims.append(m)
    res = _launch(Pa, ims)
    W = {}
    for (nm, rows, cols) in CVT:
        if rows >= 1024:
            W[nm + "_bf"] = np.concatenate(
                [np.asarray(res.results[c]["co_" + nm]) for c in range(8)], 0)
        else:
            W[nm + "_bf"] = np.asarray(res.results[0]["co_" + nm])
    qsums = np.stack([np.asarray(res.results[c]["qsums"]) for c in range(8)])
    qall = np.ascontiguousarray(
        np.concatenate([qsums[:, :, m].T for m in range(4)], axis=1))
    Pb = build_prepB()
    qinv = np.array([[1.0 / (r * co)] for (_, r, co) in QNT], np.float32)
    ims = []
    for c in range(8):
        m = {"qsums_all": qall, "qinv": qinv}
        for (nm, rows, cols) in QNT:
            rpc = rows // 8
            m["q_" + nm] = np.ascontiguousarray(prep[nm][c * rpc:(c + 1) * rpc])
        ims.append(m)
    res = _launch(Pb, ims)
    for (nm, rows, cols) in QNT:
        W[nm + "b"] = np.concatenate(
            [np.asarray(res.results[c]["qo_" + nm]) for c in range(8)], 0)
    W["wmeans"] = np.asarray(res.results[0]["wmeans"])
    return W


def run_l0(prep, W, x0, st):
    P = build_l0tok()
    ims = []
    for c in range(8):
        b, q = c >> 2, c & 3
        ims.append(dict(
            x=np.ascontiguousarray(x0[b, q * 1024:(q + 1) * 1024]),
            WxT=W["WxT_bf"], WzT=W["WzT_bf"], WBT=W["WBT_bf"], WCT=W["WCT_bf"],
            WdtdT=W["WdtdT_bf"], WdtT=W["WdtT_bf"],
            bx_c=prep["bx_c"], bz_c=prep["bz_c"], bdt_c=prep["bdt_c"],
            ident=prep["ident"]))
    res = _launch(P, ims)
    XPJ = np.zeros((1024, NTOK), np.float32)
    DLT = np.zeros((4096, NTOK), np.float32)
    BTS = np.zeros((1024, NTOK), np.float32)
    CTS = np.zeros((4096, NTOK), np.float32)
    SZT = np.zeros((4096, NTOK), np.float32)
    for c in range(8):
        g = cslice(c)
        r = res.results[c]
        XPJ[:, g:g + 1024] = r["XPJ"]
        DLT[:, g:g + 1024] = r["DLT"]
        BTS[:, g:g + 1024] = r["BTS"]
        CTS[:, g:g + 1024] = r["CTS"]
        SZT[:, g:g + 1024] = r["SZT"]
    st.update(XPJ=XPJ, DLT=DLT, BTS=BTS, CTS=CTS, SZT=SZT)

    P = build_l0scan()
    ims = []
    brow_idx = np.zeros((8, 16, 2), np.int64)
    for ct in range(8):
        for sidx in range(16):
            for j in range(2):
                brow_idx[ct, sidx, j] = sidx * 64 + 2 * ct + j
    crows = np.arange(16)[:, None] * 256 + np.arange(64)[None, :]
    for c in range(8):
        b, hb = c >> 2, c & 3
        bs = slice(b * SEQ, (b + 1) * SEQ)
        ims.append(dict(
            xpj=np.ascontiguousarray(XPJ[hb * 256:(hb + 1) * 256, bs]),
            dlt=np.ascontiguousarray(DLT[hb * 1024:(hb + 1) * 1024, bs]),
            bts=np.ascontiguousarray(BTS[(brow_idx + hb * 16).ravel()][:, bs]),
            cts=np.ascontiguousarray(CTS[(crows + hb * 64).ravel()][:, bs]),
            szt=np.ascontiguousarray(SZT[hb * 1024:(hb + 1) * 1024, bs]),
            alog=np.ascontiguousarray(prep["A_log"][hb * 1024:(hb + 1) * 1024]),
            convw=np.ascontiguousarray(prep["convw"][hb * 1024:(hb + 1) * 1024]),
            convb_c=np.ascontiguousarray(
                prep["convb"][hb * 1024:(hb + 1) * 1024].reshape(8, 128).T),
            dvec_c=np.ascontiguousarray(
                prep["Dvec"][hb * 1024:(hb + 1) * 1024].reshape(8, 128).T),
            REPB=prep["REPB"], REPC=prep["REPC"]))
    res = _launch(P, ims)
    YFT = np.zeros((4096, NTOK), dtype=BF)
    for c in range(8):
        b, hb = c >> 2, c & 3
        YFT[hb * 1024:(hb + 1) * 1024, b * SEQ:(b + 1) * SEQ] = res.results[c]["YFT"]
    st["YFT"] = YFT

    P = build_l0ffn()
    ims = []
    for c in range(8):
        b, q = c >> 2, c & 3
        g = cslice(c)
        ims.append(dict(
            x=np.ascontiguousarray(x0[b, q * 1024:(q + 1) * 1024]),
            yft=np.ascontiguousarray(YFT[:, g:g + 1024]),
            WoutT=W["WoutT_bf"], bout_r=W["bout_r_bf"], wnorm=prep["wnorm"],
            QGVb=W["QGVb"], QWOb=W["QWOb"], wmeans=W["wmeans"],
            ident=prep["ident"], ones_r=prep["ones_r"]))
    res = _launch(P, ims)
    x1 = np.zeros((BATCH, SEQ, DM), np.float32)
    for c in range(8):
        b, q = c >> 2, c & 3
        x1[b, q * 1024:(q + 1) * 1024] = res.results[c]["X1"]
    return x1


def kernel(**inputs):
    prep = host_prep(inputs)
    W = run_prep(prep)
    st = {}
    x = run_l0(prep, W, prep["x0"], st)
    x = run_l1(prep, W, x)
    x = run_l2(prep, W, x)
    x = run_l3(prep, W, x)
    return x


def make_xnT2(P, x_dram, nt, dst, pref="xn", ln=True):
    KT = DM // 128
    ntok = nt * 128
    xt = P.sbuf(pref + "_xt", [128, DM], F32)
    xnb = P.sbuf(pref + "_xnb", [128, DM], BF16)
    col = P.sbuf(pref + "_col", [128, 8], F32)
    sq = P.sbuf(pref + "_sq", [128, DM], F32)
    tok = None
    for ts in range(nt):
        tin = P.dma("sync", xt[:, :], x_dram[ts * 128:(ts + 1) * 128, 0:DM],
                    waits=[tok] if tok else ())
        if ln:
            tln = ln_tile(P, xt[:, :], xnb[:, :], DM, col, sq, waits=[tin])
        else:
            tln = P.op("vector", lambda E: ecopy(E, out=xnb[:, :],
                                                         in_=xt[:, :]),
                       waits=[tin], sig="VE")
        tok = emit_tp(P, lambda ri, cj: xnb[:, cj * 128:(cj + 1) * 128],
                      lambda ri, cj, ts=ts: dst[:, cj * ntok + ts * 128:cj * ntok + ts * 128 + 128],
                      1, KT, P.ident, P.tps, waits=[tln])
    return tok


def build_l1tok():
    P = Prog("l1tok")
    x = P.din("x", [TPC, DM])
    WqT = P.din("WqT", [DM, DM], BF16)
    WkT = P.din("WkT", [DM, DM], BF16)
    WvT = P.din("WvT", [DM, DM], BF16)
    rcos = P.din("rcos", [TPC, 64])
    rsin = P.din("rsin", [TPC, 64])
    ident = P.din("ident", [128, 128], BF16)
    QT = P.dout("QT", [DM, TPC], BF16)
    KTo = P.dout("KTo", [DM, TPC], BF16)
    V = P.dout("V", [TPC, DM], BF16)

    P.ident = P.sbuf("identsb", [128, 128], BF16)
    P.tps = P.psum("tps", [128, 128], BF16)
    ti = P.dma("sync", P.ident[:, :], ident)
    P.wait("tensor", ti)
    xnT = P.sbuf("xnT", [128, 16 * TPC], BF16)
    txn = make_xnT2(P, x, TPC // 128, xnT)
    rc = P.sbuf("rc", [128, 8 * 64], F32)
    rs = P.sbuf("rs", [128, 8 * 64], F32)
    trc_l = []
    for ts in range(8):
        trc_l.append(P.dma("sync", rc[:, ts * 64:(ts + 1) * 64],
                           rcos[ts * 128:(ts + 1) * 128, :]))
        trc_l.append(P.dma("sync", rs[:, ts * 64:(ts + 1) * 64],
                           rsin[ts * 128:(ts + 1) * 128, :]))
    qf = P.sbuf("qf", [128, 8 * DM], F32)
    wpshare = P.sbuf("wpshare", [128, 2 * 16 * 512], BF16)
    qrb = P.sbuf("qrb", [128, DM], BF16)
    tm1 = P.sbuf("tm1", [128, 1024], F32)
    tm2 = P.sbuf("tm2", [128, 1024], F32)
    outT = P.sbuf("outT", [128, 16 * TPC], BF16)
    pool = PsumPool(P, 4, [128, 512], "pp")
    vst = StagePool(P, 3, [128, 512], "vs", BF16)

    def evac_to_qf(ps, ts, ncol, waits):
        return P.op("scalar", lambda E, ps=ps, ts=ts, ncol=ncol: ecopy(E, 
            out=qf[:, ts * DM + ncol:ts * DM + ncol + 512], in_=ps[:, 0:512]),
            waits=waits, sig="SC")

    prev_call = [None]
    for (wdram, odram, name) in ((WqT, QT, "q"), (WkT, KTo, "k")):
        tmm = mm_tok(P, xnT, 16, TPC, wdram, DM, pool, evac_to_qf,
                     "m" + name, stat_tok=txn, wpan=wpshare,
                     pre_wait=prev_call[0])
        tp_tok = None
        for ts in range(8):
            sl = qf[:, ts * DM:(ts + 1) * DM]
            x1v = sl.rearrange("p (h c) -> p h c", h=16)[:, :, 0:64]
            x2v = sl.rearrange("p (h c) -> p h c", h=16)[:, :, 64:128]
            cosb = rc[:, ts * 64:(ts + 1) * 64].rearrange(
                "p (o c) -> p o c", o=1).broadcast_to([128, 16, 64])
            sinb = rs[:, ts * 64:(ts + 1) * 64].rearrange(
                "p (o c) -> p o c", o=1).broadcast_to([128, 16, 64])
            o1v = qrb[:, :].rearrange("p (h c) -> p h c", h=16)[:, :, 0:64]
            o2v = qrb[:, :].rearrange("p (h c) -> p h c", h=16)[:, :, 64:128]
            t1v = tm1[:, :].rearrange("p (h c) -> p h c", h=16)
            t2v = tm2[:, :].rearrange("p (h c) -> p h c", h=16)
            w0 = [tmm] + trc_l + ([tp_tok] if tp_tok else [])
            ta = P.op("vector", lambda E, x1v=x1v, cosb=cosb, t1v=t1v:
                      E.tensor_tensor(out=t1v, in0=x1v, in1=cosb, op=ALU.mult),
                      waits=w0, sig="VE")
            tb = P.op("vector", lambda E, x2v=x2v, sinb=sinb, t2v=t2v:
                      E.tensor_tensor(out=t2v, in0=x2v, in1=sinb, op=ALU.mult),
                      waits=w0, sig="VE")
            tc = P.op("vector", lambda E, o1v=o1v, t1v=t1v, t2v=t2v:
                      E.tensor_tensor(out=o1v, in0=t1v, in1=t2v, op=ALU.subtract),
                      waits=[ta, tb], sig="VE")
            td = P.op("vector", lambda E, x1v=x1v, sinb=sinb, t1v=t1v:
                      E.tensor_tensor(out=t1v, in0=x1v, in1=sinb, op=ALU.mult),
                      waits=[tc], sig="VE")
            te = P.op("vector", lambda E, x2v=x2v, cosb=cosb, t2v=t2v:
                      E.tensor_tensor(out=t2v, in0=x2v, in1=cosb, op=ALU.mult),
                      waits=[tc], sig="VE")
            tf = P.op("vector", lambda E, o2v=o2v, t1v=t1v, t2v=t2v:
                      E.tensor_tensor(out=o2v, in0=t1v, in1=t2v, op=ALU.add),
                      waits=[td, te], sig="VE")
            tp_tok = emit_tp(P, lambda ri, cj: qrb[:, cj * 128:(cj + 1) * 128],
                             lambda ri, cj, ts=ts: outT[:, cj * TPC + ts * 128:cj * TPC + ts * 128 + 128],
                             1, 16, P.ident, P.tps, waits=[tf])
        tout = None
        for kt in range(16):
            tout = P.dma("gpsimd", odram[kt * 128:(kt + 1) * 128, :],
                         outT[:, kt * TPC:(kt + 1) * TPC], waits=[tp_tok])
        P.final_tokens.append(tout)
        prev_call[0] = tmm

    def evac_v(ps, ts, ncol, waits):
        si, sb, sbusy = vst.get()
        te = P.op("scalar", lambda E, ps=ps, sb=sb: ecopy(E, 
            out=sb[:, :], in_=ps[:, 0:512]),
            waits=list(waits) + ([sbusy] if sbusy else []), sig="SC")
        td = P.dma("gpsimd", V[ts * 128:(ts + 1) * 128, ncol:ncol + 512],
                   sb[:, :], waits=[te])
        vst.done(si, td)
        return td

    tv = mm_tok(P, xnT, 16, TPC, WvT, DM, pool, evac_v, "mv", stat_tok=txn,
                wpan=wpshare, pre_wait=prev_call[0])
    P.final_tokens.append(tv)
    return P


def build_l1attn():
    P = Prog("l1attn")
    qT = P.din("qT", [512, SEQ], BF16)
    kT = P.din("kT", [512, SEQ], BF16)
    vp = P.din("vp", [4, SEQ, 128], BF16)
    masks = P.din("masks", [4, 128, 512], BF16)
    o1 = P.dout("o1", [4, SEQ, 128], BF16)

    qh = P.sbuf("qh", [128, SEQ], BF16)
    kh = P.sbuf("kh", [128, SEQ], BF16)
    va = P.sbuf("va", [128, 32 * 132], BF16)
    msb = P.sbuf("msb", [128, 4 * 512], BF16)
    s2f = P.sbuf("s2f", [128, 512], F32)
    s4b = P.sbuf("s4b", [128, 512], BF16)
    dn = P.sbuf("dn", [128, 2], F32)
    ost = StagePool(P, 3, [128, 128], "os", BF16)
    pss = P.psum("pss", [128, 512], F32)
    po = [P.psum(f"po{i}", [128, 129], F32) for i in range(4)]
    tmsk = [P.dma("sync", msb[:, r * 512:(r + 1) * 512], masks[r]) for r in range(4)]

    guard = None
    for hb in range(4):
        gw = [guard] if guard else []
        tq = P.dma("sync", qh[:, :], qT[hb * 128:(hb + 1) * 128, :], waits=gw)
        tk = P.dma("sync", kh[:, :], kT[hb * 128:(hb + 1) * 128, :], waits=gw)
        tva = P.op("vector", lambda E: E.memset(va[:, :], 1.0), waits=gw,
                   sig="VE")
        tvl = []
        for kb in range(32):
            tvl.append(P.dma("sync", va[:, kb * 132:kb * 132 + 128],
                             vp[hb, kb * 128:(kb + 1) * 128, :], waits=[tva]))
        prev_evac = None
        last_av = None
        prev_o = [None] * 4
        for qb in range(8):
            nkb = 4 * (qb + 1)
            for kb in range(nkb):
                tmm = P.op("tensor", lambda E, kb=kb, qb=qb: E.matmul(
                    out=pss[:, :], lhsT=kh[:, kb * 128:(kb + 1) * 128],
                    rhs=qh[:, qb * 512:(qb + 1) * 512], start=True, stop=True,
                    skip_group_check=True),
                    waits=[tq, tk] + ([prev_evac] if prev_evac else []), sig="MM")
                t1 = P.op("scalar", lambda E: E.activation(
                    out=s2f[:, :], in_=pss[:, :], func=AF.Square),
                    waits=[tmm], sig="SC")
                prev_evac = t1
                t2 = P.op("vector", lambda E: E.tensor_tensor(
                    out=s4b[:, :], in0=s2f[:, :], in1=s2f[:, :], op=ALU.mult),
                    waits=[t1] + ([last_av] if last_av else []), sig="VE")
                if kb >= qb * 4:
                    rel = kb - qb * 4
                    t2 = P.op("vector", lambda E, rel=rel: E.tensor_tensor(
                        out=s4b[:, :], in0=s4b[:, :],
                        in1=msb[:, rel * 512:(rel + 1) * 512], op=ALU.mult),
                        waits=[t2] + tmsk, sig="VE")
                for qs in range(4):
                    tav = P.op("tensor", lambda E, qs=qs, kb=kb, nkb=nkb: E.matmul(
                        out=po[qs][:, :], lhsT=s4b[:, qs * 128:(qs + 1) * 128],
                        rhs=va[:, kb * 132:kb * 132 + 129],
                        start=(kb == 0), stop=(kb == nkb - 1),
                        skip_group_check=True),
                        waits=[t2] + tvl + ([prev_o[qs]] if prev_o[qs] and kb == 0 else []),
                        sig="MM")
                    last_av = tav
                    if kb == nkb - 1:
                        prev_o[qs] = tav
            for qs in range(4):
                td1 = P.op("vector", lambda E, qs=qs: E.tensor_scalar(
                    out=dn[:, 0:1], in0=po[qs][:, 128:129], scalar1=1e-6,
                    scalar2=None, op0=ALU.max), waits=[prev_o[qs]], sig="VE")
                td2 = P.op("vector", lambda E: E.reciprocal(
                    out=dn[:, 1:2], in_=dn[:, 0:1]), waits=[td1], sig="VE")
                si, sb, sbusy = ost.get()
                te = P.op("scalar", lambda E, qs=qs, sb=sb: E.activation(
                    out=sb[:, :], in_=po[qs][:, 0:128], func=AF.Copy,
                    scale=dn[:, 1:2]),
                    waits=[td2] + ([sbusy] if sbusy else []), sig="SC")
                prev_o[qs] = te
                tdo = P.dma("gpsimd",
                            o1[hb, qb * 512 + qs * 128:qb * 512 + (qs + 1) * 128, :],
                            sb[:, :], waits=[te])
                ost.done(si, tdo)
                if qb == 7:
                    P.final_tokens.append(tdo)
                guard = tdo
    return P


def emit_combine(P, x_dram, out_dram, g_dram, s_mm_fn, pool, xst, x1st, gst,
                 pref="cmb"):
    lastx = None
    for nc in range(4):
        for ts in range(8):
            xi, xs, xb = xst.get()
            txi = P.dma("sync", xs[:, :],
                        x_dram[ts * 128:(ts + 1) * 128, nc * 512:(nc + 1) * 512],
                        waits=[xb] if xb else ())
            gi, gs_, gb_ = gst.get()
            tgi = P.dma("sync", gs_[:, :],
                        g_dram[ts * 128:(ts + 1) * 128, nc * 512:(nc + 1) * 512],
                        waits=[gb_] if gb_ else ())
            pi, ps, pb = pool.get()
            tmm = s_mm_fn(ts, nc, ps, [pb] if pb else [])
            oi, os_, ob = x1st.get()
            t1 = P.op("vector", lambda E, ps=ps, os_=os_, gs_=gs_:
                      E.tensor_tensor(out=os_[:, :], in0=ps[:, 0:512],
                                      in1=gs_[:, :], op=ALU.mult),
                      waits=[tmm, tgi] + ([ob] if ob else []), sig="VE")
            t2 = P.op("vector", lambda E, os_=os_, xs=xs: E.tensor_tensor(
                out=os_[:, :], in0=os_[:, :], in1=xs[:, :], op=ALU.add),
                waits=[t1, txi], sig="VE")
            pool.done(pi, t2)
            xst.done(xi, t2)
            gst.done(gi, t2)
            td = P.dma("gpsimd",
                       out_dram[ts * 128:(ts + 1) * 128, nc * 512:(nc + 1) * 512],
                       os_[:, :], waits=[t2])
            x1st.done(oi, td)
            lastx = td
    return lastx


def build_l1ffn():
    P = Prog("l1ffn")
    x = P.din("x", [TPC, DM])
    o1s = P.din("o1s", [TPC, DM], BF16)
    WoT = P.din("WoT", [DM, DM], BF16)
    gateT = P.din("gateT", [DM, FF1], BF16)
    upT = P.din("upT", [DM, FF1], BF16)
    downT = P.din("downT", [FF1, DM], BF16)
    ident = P.din("ident", [128, 128], BF16)
    X2 = P.dout("X2", [TPC, DM])
    GATED = P.dint("GATED", [TPC, FF1], BF16)
    G1D = P.dint("G1D", [TPC, DM], F32)

    P.ident = P.sbuf("identsb", [128, 128], BF16)
    P.tps = P.psum("tps", [128, 128], BF16)
    ti = P.dma("sync", P.ident[:, :], ident)
    P.wait("tensor", ti)
    ARB = P.sbuf("ARB", [128, 32768], BF16)
    WP = P.sbuf("WP", [128, 16384], BF16)
    wpu = P.sbuf("wpu", [128, 16 * 512], BF16)
    w1 = P.sbuf("w1", [128, 512], F32)
    w2 = P.sbuf("w2", [128, 512], F32)
    gb = P.sbuf("gb", [128, 2048], BF16)
    pool = PsumPool(P, 4, [128, 512], "pp")
    gst = StagePool(P, 2, [128, 512], "gs", BF16)
    xst = StagePool(P, 2, [128, 512], "xs", F32)
    x1st = StagePool(P, 2, [128, 512], "x1s", F32)
    gfst = StagePool(P, 2, [128, 512], "gf", F32)
    g1st = StagePool(P, 2, [128, 256], "g1s", F32)
    xnT = ARB[:, 0:16384]
    txn = make_xnT2(P, x, 8, xnT)
    pf = None
    for j in range(16):
        dts = []
        for kt in range(16):
            dts.append(P.dma("sync", WP[:, kt * 512:(kt + 1) * 512],
                             gateT[kt * 128:(kt + 1) * 128, j * 512:(j + 1) * 512],
                             waits=[pf] if kt == 0 and pf else ()))
            dts.append(P.dma("sync", wpu[:, kt * 512:(kt + 1) * 512],
                             upT[kt * 128:(kt + 1) * 128, j * 512:(j + 1) * 512]))
        ptok = dts[-1]
        for ts in range(8):
            pi, psg, pbg = pool.get()
            pj, psu, pbu = pool.get()
            w = [ptok, txn, pbg, pbu]
            tg = None
            for kt in range(16):
                tg = P.op("tensor", lambda E, kt=kt, ts=ts, psg=psg: E.matmul(
                    out=psg[:, :], lhsT=xnT[:, kt * TPC + ts * 128:kt * TPC + ts * 128 + 128],
                    rhs=WP[:, kt * 512:(kt + 1) * 512], start=(kt == 0),
                    stop=(kt == 15)), waits=w if kt == 0 else (), sig="MM")
            tu = None
            for kt in range(16):
                tu = P.op("tensor", lambda E, kt=kt, ts=ts, psu=psu: E.matmul(
                    out=psu[:, :], lhsT=xnT[:, kt * TPC + ts * 128:kt * TPC + ts * 128 + 128],
                    rhs=wpu[:, kt * 512:(kt + 1) * 512], start=(kt == 0),
                    stop=(kt == 15)), sig="MM")
            te1 = P.op("scalar", lambda E, psg=psg: E.activation(
                out=w1[:, :], in_=psg[:, :], func=AF.Silu), waits=[tu], sig="SC")
            te2 = P.op("scalar", lambda E, psu=psu: ecopy(E, 
                out=w2[:, :], in_=psu[:, :]), waits=[tu], sig="SC")
            si, sb, sbusy = gst.get()
            tg2 = P.op("vector", lambda E, sb=sb: E.tensor_tensor(
                out=sb[:, :], in0=w1[:, :], in1=w2[:, :], op=ALU.mult),
                waits=[te1, te2] + ([sbusy] if sbusy else []), sig="VE")
            pool.done(pi, tg2)
            pool.done(pj, tg2)
            td = P.dma("gpsimd", GATED[ts * 128:(ts + 1) * 128, j * 512:(j + 1) * 512],
                       sb[:, :], waits=[tg2])
            gst.done(si, td)
            pf = td
    tB = pf
    q2b = gb
    for th in range(2):
        txq2 = None
        for tsl in range(4):
            ts = th * 4 + tsl
            for cc in range(4):
                si, sb, sbusy = gst.get()
                tin = P.dma("sync", q2b[:, :],
                            GATED[ts * 128:(ts + 1) * 128, cc * 2048:(cc + 1) * 2048],
                            waits=[txq2, tB] if txq2 else [tB])
                txq2 = emit_tp(P, lambda ri, cj: q2b[:, cj * 128:(cj + 1) * 128],
                               lambda ri, cj, tsl=tsl, cc=cc:
                               ARB[:, (cc * 16 + cj) * 512 + tsl * 128:(cc * 16 + cj) * 512 + tsl * 128 + 128],
                               1, 16, P.ident, P.tps, waits=[tin])
        pfree = None
        for dc in range(8):
            dts = []
            for kt in range(64):
                dts.append(P.dma("sync", WP[:, kt * 256:(kt + 1) * 256],
                                 downT[kt * 128:(kt + 1) * 128, dc * 256:(dc + 1) * 256],
                                 waits=[pfree] if kt == 0 and pfree else ()))
            ptok = dts[-1]
            for tsl in range(4):
                ts = th * 4 + tsl
                pi, ps, pb = pool.get()
                w = [ptok, txq2] + ([pb] if pb else [])
                tmm = None
                for kt in range(64):
                    tmm = P.op("tensor", lambda E, kt=kt, tsl=tsl, ps=ps: E.matmul(
                        out=ps[:, 0:256],
                        lhsT=ARB[:, kt * 512 + tsl * 128:kt * 512 + tsl * 128 + 128],
                        rhs=WP[:, kt * 256:(kt + 1) * 256],
                        start=(kt == 0), stop=(kt == 63)),
                        waits=w if kt == 0 else (), sig="MM")
                si1, sg1, sg1b = g1st.get()
                te = P.op("scalar", lambda E, ps=ps, sg1=sg1: E.activation(
                    out=sg1[:, :], in_=ps[:, 0:256], func=AF.Silu),
                    waits=[tmm] + ([sg1b] if sg1b else []), sig="SC")
                tdg = P.dma("gpsimd",
                            G1D[ts * 128:(ts + 1) * 128, dc * 256:(dc + 1) * 256],
                            sg1[:, :], waits=[te])
                g1st.done(si1, tdg)
                pool.done(pi, te)
                pfree = tdg
    tC = pfree
    OT = ARB[:, 0:16384]
    tp_tok = tC
    for ts in range(8):
        tin = P.dma("sync", gb[:, :], o1s[ts * 128:(ts + 1) * 128, :],
                    waits=[tp_tok, tC])
        tp_tok = emit_tp(P, lambda ri, cj: gb[:, cj * 128:(cj + 1) * 128],
                         lambda ri, cj, ts=ts: OT[:, cj * TPC + ts * 128:cj * TPC + ts * 128 + 128],
                         1, 16, P.ident, P.tps, waits=[tin])
    panel_tok = {}
    prevmm = [tp_tok]

    def s1_mm(ts, nc, ps, waits):
        if nc not in panel_tok:
            dts = []
            for kt in range(16):
                dts.append(P.dma("sync", wpu[:, kt * 512:(kt + 1) * 512],
                                 WoT[kt * 128:(kt + 1) * 128, nc * 512:(nc + 1) * 512],
                                 waits=[prevmm[0]] if kt == 0 else ()))
            panel_tok[nc] = dts[-1]
        tmm = None
        w = [panel_tok[nc], tp_tok] + list(waits)
        for kt in range(16):
            tmm = P.op("tensor", lambda E, kt=kt, ts=ts, ps=ps: E.matmul(
                out=ps[:, 0:512],
                lhsT=OT[:, kt * TPC + ts * 128:kt * TPC + ts * 128 + 128],
                rhs=wpu[:, kt * 512:(kt + 1) * 512], start=(kt == 0),
                stop=(kt == 15)), waits=w if kt == 0 else (), sig="MM")
        prevmm[0] = tmm
        return tmm

    lastx = emit_combine(P, x, X2, G1D, s1_mm, pool, xst, x1st, gfst)
    P.final_tokens.append(lastx)
    return P


def build_l2tok():
    P = Prog("l2tok")
    xe = P.din("xe", [1152, DM])
    W2kT = P.din("W2kT", [DM, DM], BF16)
    W2vT = P.din("W2vT", [DM, DM], BF16)
    W2rT = P.din("W2rT", [DM, DM], BF16)
    ident = P.din("ident", [128, 128], BF16)
    ATS = P.dout("ATS", [DM, TPC])
    EKS = P.dout("EKS", [DM, TPC])
    RTS = P.dout("RTS", [DM, TPC])

    P.ident = P.sbuf("identsb", [128, 128], BF16)
    P.tps = P.psum("tps", [128, 128], BF16)
    ti = P.dma("sync", P.ident[:, :], ident)
    P.wait("tensor", ti)
    NTE = 1152
    xnT = P.sbuf("xnT", [128, 16 * NTE], BF16)
    txn = make_xnT2(P, xe, 9, xnT)
    EK = P.sbuf("EK", [128, 16 * 1024], F32)
    pool = PsumPool(P, 4, [128, 512], "pp")
    stg = StagePool(P, 4, [128, 512], "st")

    def evac_k(ps, dglob, tcol, waits):
        dt = dglob // 128
        return P.op("scalar", lambda E, ps=ps, dt=dt, tcol=tcol: E.activation(
            out=EK[:, dt * 1024 + tcol:dt * 1024 + tcol + 512], in_=ps[:, 0:512],
            func=AF.Exp), waits=waits, sig="SC")

    def mk_ev(out_dram, mul_ek):
        def evac(ps, dglob, tcol, waits):
            dt = dglob // 128
            si, sb, sbusy = stg.get()
            w = list(waits) + ([sbusy] if sbusy else [])
            if mul_ek:
                te = P.op("vector", lambda E, ps=ps, sb=sb, dt=dt, tcol=tcol:
                          E.tensor_tensor(out=sb[:, :], in0=ps[:, 0:512],
                                          in1=EK[:, dt * 1024 + tcol:dt * 1024 + tcol + 512],
                                          op=ALU.mult), waits=w + [tek], sig="VE")
            else:
                te = P.op("scalar", lambda E, ps=ps, sb=sb: ecopy(E, 
                    out=sb[:, :], in_=ps[:, 0:512]), waits=w, sig="SC")
            td = P.dma("gpsimd", out_dram[dglob:dglob + 128, tcol:tcol + 512],
                       sb[:, :], waits=[te])
            stg.done(si, td)
            return td
        return evac

    wpsh = P.sbuf("wpsh", [128, 2 * 16 * 512], BF16)
    tek = mm_dmaj(P, W2kT, DM, 16, xnT, TPC, pool, evac_k, "mk",
                  xn_tok=txn, tstride=NTE, toff=0, wpan=wpsh)
    tav = mm_dmaj(P, W2vT, DM, 16, xnT, TPC, pool, mk_ev(ATS, True), "mv",
                  xn_tok=txn, tstride=NTE, toff=0, wpan=wpsh, pre_wait=tek)
    ekst = StagePool(P, 3, [128, 512], "eks")
    tlast = None
    for dt in range(16):
        for tc in range(2):
            si, sb, sbusy = ekst.get()
            tcp = P.op("vector", lambda E, dt=dt, tc=tc, sb=sb: ecopy(E, 
                out=sb[:, :], in_=EK[:, dt * 1024 + tc * 512:dt * 1024 + (tc + 1) * 512]),
                waits=[tav] + ([sbusy] if sbusy else []), sig="VE")
            tlast = P.dma("gpsimd", EKS[dt * 128:(dt + 1) * 128, tc * 512:(tc + 1) * 512],
                          sb[:, :], waits=[tcp])
            ekst.done(si, tlast)
    trr = mm_dmaj(P, W2rT, DM, 16, xnT, TPC, pool, mk_ev(RTS, False), "mr",
                  xn_tok=txn, tstride=NTE, toff=1, wpan=wpsh, pre_wait=tav)
    P.final_tokens += [tav, tlast, trr]
    return P


def build_l2scan():
    P = Prog("l2scan")
    ats = P.din("ats", [512, SEQ])
    eks = P.din("eks", [512, SEQ])
    rts = P.din("rts", [512, SEQ])
    td_c = P.din("td_c", [128, 4])
    ident = P.din("ident", [128, 128], BF16)
    S2K = P.dout("S2K", [SEQ, 512], BF16)

    P.ident = P.sbuf("identsb", [128, 128], BF16)
    P.tps = P.psum("tps", [128, 128], BF16)
    ti = P.dma("sync", P.ident[:, :], ident)
    P.wait("tensor", ti)
    stc = P.sbuf("stc", [128, 8], F32)
    ttd = P.dma("sync", stc[:, 0:4], td_c)
    a = P.sbuf("a", [128, SEQ], F32)
    ek = P.sbuf("ek", [128, SEQ], F32)
    r = P.sbuf("r", [128, SEQ], F32)
    wt = P.sbuf("wt", [128, SEQ], F32)
    num = P.sbuf("num", [128, SEQ], F32)
    den = P.sbuf("den", [128, SEQ], F32)
    s2b = P.sbuf("s2b", [128, SEQ], BF16)
    s2T = P.sbuf("s2T", [128, SEQ], BF16)
    wc = P.sbuf("wc", [128, 2], F32)
    guard = None
    for ct in range(4):
        gw = [guard] if guard else []
        t1 = P.dma("sync", a[:, :], ats[ct * 128:(ct + 1) * 128, :], waits=gw)
        t2 = P.dma("sync", ek[:, :], eks[ct * 128:(ct + 1) * 128, :], waits=gw)
        t3 = P.dma("sync", r[:, :], rts[ct * 128:(ct + 1) * 128, :], waits=gw)
        tw1 = P.op("scalar", lambda E, ct=ct: E.activation(
            out=wc[:, 0:1], in_=stc[:, ct:ct + 1], func=AF.Exp),
            waits=[ttd] + gw, sig="SC")
        tw2 = P.op("scalar", lambda E: E.activation(
            out=wc[:, 1:2], in_=wc[:, 0:1], func=AF.Exp, scale=-1.0),
            waits=[tw1], sig="SC")
        tw3 = P.op("vector", lambda E: E.tensor_scalar(
            out=wt[:, :], in0=a[:, :], scalar1=0.0, scalar2=wc[:, 1:2],
            op0=ALU.mult, op1=ALU.add), waits=[t1, tw2], sig="VE")
        tn = P.op("vector", lambda E: E.tensor_tensor_scan(
            out=num[:, :], data0=wt[:, :], data1=a[:, :], initial=0.0,
            op0=ALU.mult, op1=ALU.add), waits=[tw3], sig="VE")
        td = P.op("vector", lambda E: E.tensor_tensor_scan(
            out=den[:, :], data0=wt[:, :], data1=ek[:, :], initial=0.0,
            op0=ALU.mult, op1=ALU.add), waits=[t2, tw3], sig="VE")
        te = P.op("vector", lambda E: E.tensor_scalar(
            out=den[:, :], in0=den[:, :], scalar1=1e-8, scalar2=None,
            op0=ALU.add), waits=[td], sig="VE")
        tq = P.op("vector", lambda E: E.reciprocal(out=den[:, :], in_=den[:, :]),
                  waits=[te], sig="VE")
        tm = P.op("vector", lambda E: E.tensor_tensor(
            out=num[:, :], in0=num[:, :], in1=den[:, :], op=ALU.mult),
            waits=[tn, tq], sig="VE")
        ts2 = P.op("vector", lambda E: E.tensor_tensor(
            out=s2b[:, :], in0=num[:, :], in1=r[:, :], op=ALU.mult),
            waits=[tm, t3], sig="VE")
        ttp = emit_tp(P, lambda ri, cj: s2b[:, cj * 128:(cj + 1) * 128],
                      lambda ri, cj: s2T[:, cj * 128:(cj + 1) * 128],
                      1, 32, P.ident, P.tps, waits=[ts2])
        for tb in range(32):
            guard = P.dma("gpsimd", S2K[tb * 128:(tb + 1) * 128, ct * 128:(ct + 1) * 128],
                          s2T[:, tb * 128:(tb + 1) * 128], waits=[ttp])
        P.final_tokens.append(guard)
    return P


def build_l2ffn():
    P = Prog("l2ffn")
    x = P.din("x", [TPC, DM])
    s2s = P.din("s2s", [TPC, DM], BF16)
    gn = P.din("gn", [DM])
    dn2 = P.din("dn2", [INT2])
    QG2 = P.din("QG2b", [DM, 2 * INT2], BF16)
    QD2 = P.din("QD2b", [INT2, DM], BF16)
    wmeans = P.din("wmeans", [4, 1])
    ident = P.din("ident", [128, 128], BF16)
    X3 = P.dout("X3", [TPC, DM])
    U2L = (P.dout if L2_DBG else P.dint)("U2L", [TPC, INT2], BF16)
    G2D = P.dint("G2D", [TPC, DM], F32)

    P.ident = P.sbuf("identsb", [128, 128], BF16)
    P.tps = P.psum("tps", [128, 128], BF16)
    ti = P.dma("sync", P.ident[:, :], ident)
    P.wait("tensor", ti)
    col = P.sbuf("col", [128, 16], F32)
    c1s = P.sbuf("c1s", [128, 8], F32)
    c2s = P.sbuf("c2s", [128, 8], F32)
    m2s = P.sbuf("m2s", [128, 8], F32)
    sumsq = P.sbuf("sumsq", [128, 8], F32)
    mgv = P.sbuf("mgv", [128, 2], F32)
    tm1 = P.dma("sync", mgv[:, 0:1], wmeans[2:3, 0:1].broadcast_to([128, 1]))
    tm2 = P.dma("sync", mgv[:, 1:2], wmeans[3:4, 0:1].broadcast_to([128, 1]))
    ARB = P.sbuf("ARB", [128, 44 * 512], BF16)
    AWK = P.sbuf("AWK", [128, 11264], F32)
    WRK = P.sbuf("WRK", [128, 6144], F32)
    PAN = P.sbuf("PAN", [128, 16384], BF16)
    wpv = PAN[:, 0:8192]
    wpg = PAN[:, 8192:16384]
    WP = PAN[:, 0:11264]
    xqb = P.sbuf("xqb", [128, DM], BF16)
    ubc = P.sbuf("ubc", [128, 1408], BF16)
    q2b = P.sbuf("q2b", [128, 1408], BF16)
    u2st = StagePool(P, 2, [128, 512], "u2s", BF16)
    xst = StagePool(P, 2, [128, 512], "xs", F32)
    sst = StagePool(P, 2, [128, 512], "ss", BF16)
    gfst = StagePool(P, 2, [128, 512], "gf", F32)
    g1st = StagePool(P, 2, [128, 256], "g1s", F32)
    x1st = StagePool(P, 2, [128, 512], "x1s", F32)
    pool = PsumPool(P, 4, [128, 512], "pp")
    tmz = P.op("vector", lambda E: E.memset(m2s[:, :], 0.0), sig="VE")
    tsq0 = P.op("vector", lambda E: E.memset(sumsq[:, :], 0.0), sig="VE")
    tss2_prev = [None]
    tc1_l = [None] * 8
    wnb = AWK[:, 0:2048]
    twn = bcast_row(P, wnb, gn, DM)
    xt = AWK[:, 2048:4096]
    xnf = AWK[:, 4096:6144]
    sq = AWK[:, 6144:8192]
    hb = AWK[:, 8192:10240]
    xqT = ARB[:, 0:16384]
    tok = None
    for ts in range(8):
        tin = P.dma("sync", xt, x[ts * 128:(ts + 1) * 128, :],
                    waits=[tok] if tok else ())
        tln = ln_tile(P, xt, xnf, DM, col, sq, waits=[tin])
        ts1 = P.op("scalar", lambda E: E.activation(
            out=sq, in_=xnf, func=AF.Square, accum_out=col[:, 4:5]),
            waits=[tln], sig="SC")
        ceps, tcep = P.ccol(EPS)
        ts2a = P.op("scalar", lambda E: E.activation(
            out=col[:, 5:6], in_=col[:, 4:5], func=AF.Sqrt,
            scale=1.0 / DM, bias=ceps), waits=[ts1, tcep], sig="SC")
        ts2 = P.op("vector", lambda E: E.reciprocal(
            out=col[:, 5:6], in_=col[:, 5:6]), waits=[ts2a], sig="VE")
        ts3 = P.op("vector", lambda E: E.tensor_scalar(
            out=hb, in0=xnf, scalar1=col[:, 5:6], scalar2=None, op0=ALU.mult),
            waits=[ts2], sig="VE")
        ts4 = P.op("vector", lambda E: E.tensor_tensor(
            out=hb, in0=hb, in1=wnb, op=ALU.mult), waits=[ts3, twn], sig="VE")
        tq = quant_cols(P, hb, DM, col, 6, waits=[ts4])
        tc1_l[ts] = P.op("vector", lambda E, ts=ts: E.tensor_scalar(
            out=c1s[:, ts:ts + 1], in0=col[:, 6:7], scalar1=mgv[:, 0:1],
            scalar2=1.0 / 127.0, op0=ALU.mult, op1=ALU.mult),
            waits=[tq, tm1], sig="VE")
        trc = round_clip(P, xqb[:, :], hb, col[:, 7:8], -128, 127,
                         waits=[tq] + ([tok] if tok else []))
        tok = emit_tp(P, lambda ri, cj: xqb[:, cj * 128:(cj + 1) * 128],
                      lambda ri, cj, ts=ts: xqT[:, cj * 1024 + ts * 128:cj * 1024 + ts * 128 + 128],
                      1, 16, P.ident, P.tps, waits=[trc])
    txq = tok
    w1 = WRK[:, 0:512]
    w2 = WRK[:, 512:1024]
    pf = None
    for j in range(11):
        dts = []
        for kt in range(16):
            dts.append(P.dma("sync", wpg[:, kt * 512:(kt + 1) * 512],
                             QG2[kt * 128:(kt + 1) * 128, j * 512:(j + 1) * 512],
                             waits=[pf] if kt == 0 and pf else ()))
        for kt in range(16):
            dts.append(P.dma("sync", wpv[:, kt * 512:(kt + 1) * 512],
                             QG2[kt * 128:(kt + 1) * 128, INT2 + j * 512:INT2 + (j + 1) * 512]))
        ptok = dts[-1]
        for ts in range(8):
            pi, psg, pbg = pool.get()
            pj, psu, pbu = pool.get()
            w = [ptok, txq, pbg, pbu]
            for kt in range(16):
                P.op("tensor", lambda E, kt=kt, ts=ts, psg=psg: E.matmul(
                    out=psg[:, :], lhsT=xqT[:, kt * 1024 + ts * 128:kt * 1024 + ts * 128 + 128],
                    rhs=wpg[:, kt * 512:(kt + 1) * 512], start=(kt == 0),
                    stop=(kt == 15)), waits=w if kt == 0 else (), sig="MM")
            tmm2 = None
            for kt in range(16):
                tmm2 = P.op("tensor", lambda E, kt=kt, ts=ts, psu=psu: E.matmul(
                    out=psu[:, :], lhsT=xqT[:, kt * 1024 + ts * 128:kt * 1024 + ts * 128 + 128],
                    rhs=wpv[:, kt * 512:(kt + 1) * 512], start=(kt == 0),
                    stop=(kt == 15)), sig="MM")
            te1 = P.op("scalar", lambda E, ts=ts, psg=psg: E.activation(
                out=w1, in_=psg[:, :], func=AF.Silu, scale=c1s[:, ts:ts + 1]),
                waits=[tmm2, tc1_l[ts]], sig="SC")
            te2 = P.op("scalar", lambda E, ts=ts, psu=psu: E.activation(
                out=w2, in_=psu[:, :], func=AF.Copy, scale=c1s[:, ts:ts + 1]),
                waits=[tmm2], sig="SC")
            si, ub, ubusy = u2st.get()
            tu2 = P.op("vector", lambda E, ub=ub: E.tensor_tensor(
                out=ub[:, :], in0=w1, in1=w2, op=ALU.mult),
                waits=[te1, te2] + ([ubusy] if ubusy else []), sig="VE")
            pool.done(pi, tu2)
            pool.done(pj, tu2)
            tss = P.op("scalar", lambda E, ub=ub: E.activation(
                out=w1, in_=ub[:, :], func=AF.Square, accum_out=col[:, 8:9]),
                waits=[tu2] + ([tss2_prev[0]] if tss2_prev[0] else []),
                sig="SC")
            tss2 = P.op("vector", lambda E, ts=ts: E.tensor_tensor(
                out=sumsq[:, ts:ts + 1], in0=sumsq[:, ts:ts + 1],
                in1=col[:, 8:9], op=ALU.add), waits=[tss, tsq0], sig="VE")
            tss2_prev[0] = tss2
            tdu = P.dma("gpsimd", U2L[ts * 128:(ts + 1) * 128, j * 512:(j + 1) * 512],
                        ub[:, :], waits=[tu2])
            u2st.done(si, tdu)
            if ts == 7:
                pf = tdu
    tB = pf
    trstd = []
    for ts in range(8):
        ceps2, tcep2 = P.ccol(EPS)
        ta_ = P.op("scalar", lambda E, ts=ts: E.activation(
            out=sumsq[:, ts:ts + 1], in_=sumsq[:, ts:ts + 1], func=AF.Sqrt,
            scale=1.0 / INT2, bias=ceps2), waits=[tB, tcep2], sig="SC")
        trstd.append(P.op("vector", lambda E, ts=ts: E.reciprocal(
            out=sumsq[:, ts:ts + 1], in_=sumsq[:, ts:ts + 1]),
            waits=[ta_], sig="VE"))
    hrow = AWK[:, 0:INT2]
    dnb = AWK[:, INT2:2 * INT2]
    tdn = bcast_row(P, dnb, dn2, INT2, waits=[tB])
    q2f = WRK[:, 0:1408]
    pfree = None
    for th in range(2):
        txq2 = None
        for tsl in range(4):
            ts = th * 4 + tsl
            tldl = []
            tprev = None
            for cc in range(4):
                wv = [tB] + ([txq2] if txq2 else []) + ([tprev] if tprev else [])
                tld = P.dma("sync", ubc[:, :],
                            U2L[ts * 128:(ts + 1) * 128, cc * 1408:(cc + 1) * 1408],
                            waits=wv)
                t1 = P.op("vector", lambda E, ts=ts, cc=cc: E.tensor_scalar(
                    out=hrow[:, cc * 1408:(cc + 1) * 1408], in0=ubc[:, :],
                    scalar1=sumsq[:, ts:ts + 1], scalar2=None, op0=ALU.mult),
                    waits=[tld, trstd[ts]], sig="VE")
                tprev = P.op("vector", lambda E, cc=cc: E.tensor_tensor(
                    out=hrow[:, cc * 1408:(cc + 1) * 1408],
                    in0=hrow[:, cc * 1408:(cc + 1) * 1408],
                    in1=dnb[:, cc * 1408:(cc + 1) * 1408], op=ALU.mult),
                    waits=[t1, tdn], sig="VE")
                tldl.append(tprev)
            tq = quant_cols(P, hrow[:, :], INT2, col, 10, waits=tldl)
            tc2 = P.op("vector", lambda E, ts=ts: E.tensor_scalar(
                out=c2s[:, ts:ts + 1], in0=col[:, 10:11], scalar1=mgv[:, 1:2],
                scalar2=1.0 / 127.0, op0=ALU.mult, op1=ALU.mult),
                waits=[tq, tm2], sig="VE")
            for cc in range(4):
                trc = round_clip(P, q2f, hrow[:, cc * 1408:(cc + 1) * 1408],
                                 col[:, 11:12], -128, 127, waits=[tq])
                tcb = P.op("vector", lambda E: ecopy(E, 
                    out=q2b[:, :], in_=q2f),
                    waits=[trc] + ([txq2] if txq2 else []), sig="VE")
                txq2 = emit_tp(P, lambda ri, cj: q2b[:, cj * 128:(cj + 1) * 128],
                               lambda ri, cj, tsl=tsl, cc=cc:
                               ARB[:, (cc * 11 + cj) * 512 + tsl * 128:(cc * 11 + cj) * 512 + tsl * 128 + 128],
                               1, 11, P.ident, P.tps, waits=[tcb])
        for dc in range(8):
            dts = []
            for kt in range(44):
                dts.append(P.dma("sync", WP[:, kt * 256:(kt + 1) * 256],
                                 QD2[kt * 128:(kt + 1) * 128, dc * 256:(dc + 1) * 256],
                                 waits=[pfree, txq2] if kt == 0 and pfree else ([txq2] if kt == 0 else ())))
            ptok = dts[-1]
            for tsl in range(4):
                ts = th * 4 + tsl
                pi, ps, pb = pool.get()
                w = [ptok, txq2] + ([pb] if pb else [])
                tmm = None
                for kt in range(44):
                    tmm = P.op("tensor", lambda E, kt=kt, tsl=tsl, ps=ps: E.matmul(
                        out=ps[:, 0:256],
                        lhsT=ARB[:, kt * 512 + tsl * 128:kt * 512 + tsl * 128 + 128],
                        rhs=WP[:, kt * 256:(kt + 1) * 256],
                        start=(kt == 0), stop=(kt == 43)),
                        waits=w if kt == 0 else (), sig="MM")
                si1, sg1, sg1b = g1st.get()
                te0 = P.op("scalar", lambda E, ts=ts, ps=ps: E.activation(
                    out=ps[:, 0:256], in_=ps[:, 0:256], func=AF.Copy,
                    scale=c2s[:, ts:ts + 1]), waits=[tmm, tc2], sig="SC")
                te = P.op("scalar", lambda E, ps=ps, sg1=sg1: E.activation(
                    out=sg1[:, :], in_=ps[:, 0:256], func=AF.Silu),
                    waits=[te0] + ([sg1b] if sg1b else []), sig="SC")
                tdg = P.dma("gpsimd",
                            G2D[ts * 128:(ts + 1) * 128, dc * 256:(dc + 1) * 256],
                            sg1[:, :], waits=[te])
                g1st.done(si1, tdg)
                pool.done(pi, te)
                pfree = tdg
    tC = pfree
    lastx = None
    for nc in range(4):
        for ts in range(8):
            xi, xs, xb = xst.get()
            txi = P.dma("sync", xs[:, :],
                        x[ts * 128:(ts + 1) * 128, nc * 512:(nc + 1) * 512],
                        waits=[xb] if xb else ())
            si2, s2c, s2b_ = sst.get()
            ts2 = P.dma("sync", s2c[:, :],
                        s2s[ts * 128:(ts + 1) * 128, nc * 512:(nc + 1) * 512],
                        waits=[s2b_] if s2b_ else ())
            gi, gc, gb_ = gfst.get()
            tgi = P.dma("sync", gc[:, :],
                        G2D[ts * 128:(ts + 1) * 128, nc * 512:(nc + 1) * 512],
                        waits=[tC] + ([gb_] if gb_ else []))
            oi, os_, ob = x1st.get()
            t1 = P.op("vector", lambda E, s2c=s2c, gc=gc, os_=os_:
                      E.tensor_tensor(out=os_[:, :], in0=gc[:, :],
                                      in1=s2c[:, :], op=ALU.mult),
                      waits=[ts2, tgi] + ([ob] if ob else []), sig="VE")
            t2 = P.op("vector", lambda E, os_=os_, xs=xs: E.tensor_tensor(
                out=os_[:, :], in0=os_[:, :], in1=xs[:, :], op=ALU.add),
                waits=[t1, txi], sig="VE")
            xst.done(xi, t2)
            sst.done(si2, t2)
            gfst.done(gi, t2)
            td = P.dma("gpsimd",
                       X3[ts * 128:(ts + 1) * 128, nc * 512:(nc + 1) * 512],
                       os_[:, :], waits=[t2])
            x1st.done(oi, td)
            lastx = td
    P.final_tokens.append(lastx)
    return P


def cos_dev2(P, phT_dram, amT_dram, wT_sb, f1, f2, f3, waits=()):
    f1, f2, f3 = f1[:, :], f2[:, :], f3[:, :]
    t1 = P.dma("sync", f1, phT_dram, waits=waits)
    t2 = P.dma("sync", f2, amT_dram, waits=waits)
    t3 = P.op("vector", lambda E: E.tensor_scalar(
        out=f1, in0=f1, scalar1=-1.0 / (2 * np.pi), scalar2=0.25,
        op0=ALU.mult, op1=ALU.add), waits=[t1], sig="VE")
    t4 = P.op("vector", lambda E: E.tensor_scalar(
        out=f3, in0=f1, scalar1=RND, scalar2=-RND, op0=ALU.add, op1=ALU.add),
        waits=[t3], sig="VE")
    t5 = P.op("vector", lambda E: E.tensor_tensor(
        out=f1, in0=f1, in1=f3, op=ALU.subtract), waits=[t4], sig="VE")
    t6 = P.op("scalar", lambda E: E.activation(
        out=f3, in_=f1, func=AF.Sin, scale=2 * np.pi), waits=[t5], sig="SC")
    t7 = P.op("vector", lambda E: E.tensor_tensor(
        out=wT_sb, in0=f3, in1=f2, op=ALU.mult), waits=[t6, t2], sig="VE")
    return t7


def build_l3tok():
    P = Prog("l3tok")
    x = P.din("x", [TPC, DM])
    ident = P.din("ident", [128, 128], BF16)
    basT, phT, amT = {}, {}, {}
    for nm in ("q", "k", "v"):
        basT[nm] = P.din(f"bas{nm}T", [DM, HOLO], BF16)
        phT[nm] = P.din(f"ph{nm}T", [HOLO, DM])
        amT[nm] = P.din(f"am{nm}T", [HOLO, DM])
    QT = P.dout("QT", [DM, TPC], BF16)
    KTo = P.dout("KTo", [DM, TPC], BF16)
    V = P.dout("V", [TPC, DM], BF16)

    P.ident = P.sbuf("identsb", [128, 128], BF16)
    P.tps = P.psum("tps", [128, 128], BF16)
    ti = P.dma("sync", P.ident[:, :], ident)
    P.wait("tensor", ti)
    xnT = P.sbuf("xnT", [128, 16 * TPC], BF16)
    txn = make_xnT2(P, x, 8, xnT)
    f1 = P.sbuf("f1", [64, DM], F32)
    f2 = P.sbuf("f2", [64, DM], F32)
    f3 = P.sbuf("f3", [64, DM], F32)
    wTs = P.sbuf("wTs", [128, DM], BF16)
    bsb = P.sbuf("bsb", [128, 16 * 64], BF16)
    resT = P.sbuf("resT", [128, TPC], BF16)
    tpad1 = P.op("vector", lambda E: E.memset(wTs[64:128, :], 0.0), sig="VE")
    tpad2 = P.op("vector", lambda E: E.memset(resT[64:128, :], 0.0), sig="VE")
    psr = P.psum("psr", [64, 512], F32)
    pool = PsumPool(P, 4, [128, 512], "pp")
    ost = StagePool(P, 3, [128, 512], "os", BF16)
    outs = {"q": QT, "k": KTo, "v": V}
    if L3_CUT <= 1:
        dummyb = P.sbuf("dummyb", [128, 512], BF16)
        tz = P.op("vector", lambda E: E.memset(dummyb[:, :], 0.0),
                  waits=[txn], sig="VE")
        for od in (QT, KTo, V):
            P.final_tokens.append(P.dma("gpsimd", od[0:128, 0:512],
                                        dummyb[:, :], waits=[tz]))
        return P
    prevw = None
    for nm in ("q", "k", "v"):
        tw = cos_dev2(P, phT[nm], amT[nm], wTs[0:64, :], f1, f2, f3,
                      waits=[prevw, tpad1, tpad2] if prevw else (tpad1, tpad2))
        tbl = []
        for kt in range(16):
            tbl.append(P.dma("sync", bsb[:, kt * 64:(kt + 1) * 64],
                             basT[nm][kt * 128:(kt + 1) * 128, :],
                             waits=[prevw] if prevw else ()))
        tres_l = []
        for tc in range(2):
            tmm = None
            for kt in range(16):
                tmm = P.op("tensor", lambda E, kt=kt, tc=tc: E.matmul(
                    out=psr[:, :], lhsT=bsb[:, kt * 64:(kt + 1) * 64],
                    rhs=xnT[:, kt * TPC + tc * 512:kt * TPC + tc * 512 + 512],
                    start=(kt == 0), stop=(kt == 15)),
                    waits=[txn] + tbl + tres_l if kt == 0 else (), sig="MM")
            tres_l.append(P.op("scalar", lambda E, tc=tc: ecopy(E, 
                out=resT[0:64, tc * 512:(tc + 1) * 512], in_=psr[:, :]),
                waits=[tmm], sig="SC"))
        if L3_CUT == 2:
            prevw = tres_l[-1]
            P.final_tokens.append(prevw)
            continue
        if L3_CUT == 3 and nm != "v":
            prevw = tres_l[-1]
            P.final_tokens.append(prevw)
            continue
        if L3_CUT == 4 and nm == "v":
            prevw = tres_l[-1]
            P.final_tokens.append(prevw)
            continue
        if nm == "v":
            lastd = None
            for ts in range(8):
                for ncx in range(4):
                    pi, ps, pb = pool.get()
                    tmm = P.op("tensor", lambda E, ts=ts, ncx=ncx, ps=ps: E.matmul(
                        out=ps[:, :], lhsT=resT[:, ts * 128:ts * 128 + 128],
                        rhs=wTs[:, ncx * 512:(ncx + 1) * 512],
                        start=True, stop=True, skip_group_check=True),
                        waits=[tw] + tres_l + ([pb] if pb else []), sig="MM")
                    si, sb, sbusy = ost.get()
                    te = P.op("scalar", lambda E, ps=ps, sb=sb: ecopy(E, 
                        out=sb[:, :], in_=ps[:, :]),
                        waits=[tmm] + ([sbusy] if sbusy else []), sig="SC")
                    pool.done(pi, te)
                    lastd = P.dma("gpsimd",
                                  V[ts * 128:(ts + 1) * 128, ncx * 512:(ncx + 1) * 512],
                                  sb[:, :], waits=[te])
                    ost.done(si, lastd)
            prevw = lastd
        else:
            lastd = None
            for ds in range(16):
                for tc in range(2):
                    pi, ps, pb = pool.get()
                    tmm = P.op("tensor", lambda E, ds=ds, tc=tc, ps=ps: E.matmul(
                        out=ps[:, :], lhsT=wTs[:, ds * 128:(ds + 1) * 128],
                        rhs=resT[:, tc * 512:(tc + 1) * 512],
                        start=True, stop=True, skip_group_check=True),
                        waits=[tw] + tres_l + ([pb] if pb else []), sig="MM")
                    si, sb, sbusy = ost.get()
                    te = P.op("scalar", lambda E, ps=ps, sb=sb: ecopy(E, 
                        out=sb[:, :], in_=ps[:, :]),
                        waits=[tmm] + ([sbusy] if sbusy else []), sig="SC")
                    pool.done(pi, te)
                    if False and L3_CUT == 5:
                        ost.done(si, te)
                        lastd = te
                        continue
                    lastd = P.dma("gpsimd",
                                  outs[nm][ds * 128:(ds + 1) * 128, tc * 512:(tc + 1) * 512],
                                  sb[:, :], waits=[te])
                    ost.done(si, lastd)
            prevw = lastd
        P.final_tokens.append(prevw)
    return P


def build_l3attn():
    P = Prog("l3attn")
    qT = P.din("qT", [512, SEQ], BF16)
    kT = P.din("kT", [512, SEQ], BF16)
    vp = P.din("vp", [4, SEQ, 128], BF16)
    o3 = P.dout("o3", [4, SEQ, 128], BF16)

    qh = P.sbuf("qh", [128, SEQ], BF16)
    kh = P.sbuf("kh", [128, SEQ], BF16)
    va = P.sbuf("va", [128, 32 * 132], BF16)
    srow = P.sbuf("srow", [128, SEQ], F32)
    esb = P.sbuf("esb", [128, 512], BF16)
    esT = P.sbuf("esT", [128, 512], BF16)
    mrow = P.sbuf("mrow", [128, 4], F32)
    dn = P.sbuf("dn", [128, 2], F32)
    ost = StagePool(P, 3, [128, 128], "os", BF16)
    pss = [P.psum(f"pss{i}", [128, 512], F32) for i in range(2)]
    tpe = P.psum("tpe", [128, 128], BF16)
    po = P.psum("po", [128, 129], F32)
    sid = P.sbuf("sid", [128, 128], BF16)
    ident = P.din("ident", [128, 128], BF16)
    tid = P.dma("sync", sid[:, :], ident)
    P.wait("tensor", tid)

    guard = None
    for hb in range(4):
        gw = [guard] if guard else []
        tq = P.dma("sync", qh[:, :], qT[hb * 128:(hb + 1) * 128, :], waits=gw)
        tk = P.dma("sync", kh[:, :], kT[hb * 128:(hb + 1) * 128, :], waits=gw)
        tva = P.op("vector", lambda E: E.memset(va[:, :], 1.0), waits=gw,
                   sig="VE")
        tvl = []
        for kb in range(32):
            tvl.append(P.dma("sync", va[:, kb * 132:kb * 132 + 128],
                             vp[hb, kb * 128:(kb + 1) * 128, :], waits=[tva]))
        pse = [None, None]
        prev_exp = None
        prev_av = None
        for qs in range(32):
            tmms = []
            for kc in range(8):
                pi = kc % 2
                tmm = P.op("tensor", lambda E, qs=qs, kc=kc, pi=pi: E.matmul(
                    out=pss[pi][:, :], lhsT=qh[:, qs * 128:(qs + 1) * 128],
                    rhs=kh[:, kc * 512:(kc + 1) * 512], start=True, stop=True,
                    skip_group_check=True),
                    waits=[tq, tk] + ([pse[pi]] if pse[pi] else []), sig="MM")
                tcp = P.op("vector", lambda E, kc=kc, pi=pi: ecopy(E, 
                    out=srow[:, kc * 512:(kc + 1) * 512], in_=pss[pi][:, :]),
                    waits=[tmm], sig="VE")
                pse[pi] = tcp
                tmms.append(tcp)
            tmx = P.op("vector", lambda E: E.tensor_reduce(
                out=mrow[:, 0:1], in_=srow[:, :], axis=AX.X, op=ALU.max),
                waits=tmms, sig="VE")
            tnb = P.op("vector", lambda E: E.tensor_scalar(
                out=mrow[:, 1:2], in0=mrow[:, 0:1], scalar1=-ISQ128,
                scalar2=None, op0=ALU.mult), waits=[tmx], sig="VE")
            for kb in range(32):
                texp = P.op("scalar", lambda E, kb=kb: E.activation(
                    out=esb[:, 0:128], in_=srow[:, kb * 128:(kb + 1) * 128],
                    func=AF.Exp, scale=ISQ128, bias=mrow[:, 1:2]),
                    waits=[tnb] + ([prev_exp] if prev_exp else []), sig="SC")
                ttp = P.op("tensor", lambda E: E.transpose(
                    out=tpe[:, :], in_=esb[:, 0:128], identity=sid[:, :]),
                    waits=[texp], sig="MM")
                tev = P.op("scalar", lambda E: ecopy(E, 
                    out=esT[:, 0:128], in_=tpe[:, :]), waits=[ttp], sig="SC")
                prev_exp = tev
                tav = P.op("tensor", lambda E, kb=kb: E.matmul(
                    out=po[:, :], lhsT=esT[:, 0:128],
                    rhs=va[:, kb * 132:kb * 132 + 129],
                    start=(kb == 0), stop=(kb == 31), skip_group_check=True),
                    waits=[tev] + tvl + ([prev_av] if prev_av and kb == 0 else []),
                    sig="MM")
            td1 = P.op("vector", lambda E: E.reciprocal(
                out=dn[:, 0:1], in_=po[:, 128:129]), waits=[tav], sig="VE")
            si, sb, sbusy = ost.get()
            te = P.op("scalar", lambda E, sb=sb: E.activation(
                out=sb[:, :], in_=po[:, 0:128], func=AF.Copy,
                scale=dn[:, 0:1]), waits=[td1] + ([sbusy] if sbusy else []),
                sig="SC")
            prev_av = te
            tdo = P.dma("gpsimd", o3[hb, qs * 128:(qs + 1) * 128, :],
                        sb[:, :], waits=[te])
            ost.done(si, tdo)
            guard = tdo
            if qs == 31:
                P.final_tokens.append(tdo)
    return P


def build_l3ffn():
    P = Prog("l3ffn")
    x = P.din("x", [TPC, DM])
    o3s = P.din("o3s", [TPC, DM], BF16)
    basoT = P.din("basoT", [DM, HOLO], BF16)
    phoT = P.din("phoT", [HOLO, DM])
    amoT = P.din("amoT", [HOLO, DM])
    Wr3 = P.din("Wr3b", [DM, NE], BF16)
    We3r = P.din("We3rb", [NE * DM, DM], BF16)
    be3 = P.din("be3b", [1, NE * DM], BF16)
    ones_r = P.din("ones_r", [1, 128], BF16)
    ident = P.din("ident", [128, 128], BF16)
    X4 = P.dout("X4", [TPC, DM])

    P.ident = P.sbuf("identsb", [128, 128], BF16)
    P.tps = P.psum("tps", [128, 128], BF16)
    ti = P.dma("sync", P.ident[:, :], ident)
    P.wait("tensor", ti)
    sones = P.sbuf("sones", [1, 128], BF16)
    tio = P.dma("sync", sones[:, :], ones_r)
    sbe = P.sbuf("sbe", [1, NE * DM], BF16)
    tbe = P.dma("sync", sbe[:, :], be3)
    swr = P.sbuf("swr", [128, 16 * NE], BF16)
    twr = []
    for kt in range(16):
        twr.append(P.dma("sync", swr[:, kt * NE:(kt + 1) * NE],
                         Wr3[kt * 128:(kt + 1) * 128, :]))
    xnT = P.sbuf("xnT", [128, 16 * TPC], BF16)
    txn = make_xnT2(P, x, 8, xnT)
    ACC = P.sbuf("ACC", [128, 16384], F32)
    wpe = P.sbuf("wpe", [128, 16 * 512], BF16)
    gar = P.sbuf("gar", [128, 8 * NE], F32)
    lg = P.sbuf("lg", [128, NE], F32)
    e1 = P.sbuf("e1", [128, NE], F32)
    e2 = P.sbuf("e2", [128, NE], F32)
    cc_ = P.sbuf("cc_", [128, 8], F32)
    pool = PsumPool(P, 4, [128, 512], "pp")
    psl = P.psum("psl", [128, NE], F32)
    tg_l = []
    for ts in range(8):
        tmm = None
        for kt in range(16):
            tmm = P.op("tensor", lambda E, kt=kt, ts=ts: E.matmul(
                out=psl[:, :], lhsT=xnT[:, kt * TPC + ts * 128:kt * TPC + ts * 128 + 128],
                rhs=swr[:, kt * NE:(kt + 1) * NE], start=(kt == 0),
                stop=(kt == 15)),
                waits=[txn] + twr + tg_l[-1:] if kt == 0 else (), sig="MM")
        t0 = P.op("vector", lambda E: ecopy(E, out=lg[:, :], in_=psl[:, :]),
                  waits=[tmm], sig="VE")
        t1 = P.op("vector", lambda E: E.tensor_reduce(
            out=cc_[:, 0:1], in_=lg[:, :], axis=AX.X, op=ALU.max),
            waits=[t0], sig="VE")
        t2 = P.op("vector", lambda E: E.tensor_scalar(
            out=e1[:, :], in0=lg[:, :], scalar1=cc_[:, 0:1], scalar2=None,
            op0=ALU.is_equal), waits=[t1], sig="VE")
        t3 = P.op("vector", lambda E: E.scalar_tensor_tensor(
            out=e2[:, :], in0=e1[:, :], scalar=-1e38, in1=lg[:, :],
            op0=ALU.mult, op1=ALU.add), waits=[t2], sig="VE")
        t4 = P.op("vector", lambda E: E.tensor_reduce(
            out=cc_[:, 1:2], in_=e2[:, :], axis=AX.X, op=ALU.max),
            waits=[t3], sig="VE")
        t5 = P.op("vector", lambda E: E.tensor_scalar(
            out=e2[:, :], in0=e2[:, :], scalar1=cc_[:, 1:2], scalar2=None,
            op0=ALU.is_equal), waits=[t4], sig="VE")
        t6 = P.op("vector", lambda E: E.tensor_tensor(
            out=cc_[:, 2:3], in0=cc_[:, 1:2], in1=cc_[:, 0:1], op=ALU.subtract),
            waits=[t5], sig="VE")
        t7 = P.op("scalar", lambda E: E.activation(
            out=cc_[:, 3:4], in_=cc_[:, 2:3], func=AF.Exp), waits=[t6], sig="SC")
        t8 = P.op("vector", lambda E: E.tensor_scalar(
            out=cc_[:, 4:5], in0=cc_[:, 3:4], scalar1=1.0, scalar2=None,
            op0=ALU.add), waits=[t7], sig="VE")
        t9 = P.op("vector", lambda E: E.reciprocal(
            out=cc_[:, 5:6], in_=cc_[:, 4:5]), waits=[t8], sig="VE")
        t10 = P.op("vector", lambda E: E.tensor_scalar(
            out=cc_[:, 6:7], in0=cc_[:, 5:6], scalar1=-1.0, scalar2=1.0,
            op0=ALU.mult, op1=ALU.add), waits=[t9], sig="VE")
        t11 = P.op("vector", lambda E: E.tensor_scalar(
            out=e1[:, :], in0=e1[:, :], scalar1=cc_[:, 5:6], scalar2=None,
            op0=ALU.mult), waits=[t10], sig="VE")
        t12 = P.op("vector", lambda E: E.tensor_scalar(
            out=e2[:, :], in0=e2[:, :], scalar1=cc_[:, 6:7], scalar2=None,
            op0=ALU.mult), waits=[t11], sig="VE")
        t13 = P.op("vector", lambda E, ts=ts: E.tensor_tensor(
            out=gar[:, ts * NE:(ts + 1) * NE], in0=e1[:, :], in1=e2[:, :],
            op=ALU.add), waits=[t12], sig="VE")
        tg_l.append(t13)
    lastacc = {}
    pf = None
    for e in range(NE):
        for ncx in range(4):
            buf = 0
            dts = []
            for kt in range(16):
                dts.append(P.dma("sync",
                                 wpe[:, (buf * 16 + kt) * 512:(buf * 16 + kt + 1) * 512],
                                 We3r[e * DM + kt * 128:e * DM + (kt + 1) * 128,
                                      ncx * 512:(ncx + 1) * 512],
                                 waits=[pf] if kt == 0 and pf else ()))
            ptok = dts[-1]
            for ts in range(8):
                pi, ps, pb = pool.get()
                w = [ptok, txn, tg_l[ts]] + ([pb] if pb else [])
                for kt in range(16):
                    P.op("tensor", lambda E, kt=kt, ts=ts, buf=buf, ps=ps: E.matmul(
                        out=ps[:, :],
                        lhsT=xnT[:, kt * TPC + ts * 128:kt * TPC + ts * 128 + 128],
                        rhs=wpe[:, (buf * 16 + kt) * 512:(buf * 16 + kt + 1) * 512],
                        start=(kt == 0), stop=False),
                        waits=w if kt == 0 else (), sig="MM")
                tmm = P.op("tensor", lambda E, e=e, ncx=ncx, ps=ps: E.matmul(
                    out=ps[:, :], lhsT=sones[0:1, 0:128],
                    rhs=sbe[0:1, e * DM + ncx * 512:e * DM + (ncx + 1) * 512],
                    start=False, stop=True), waits=[tio, tbe], sig="MM")
                key = (ts, ncx)
                if e == 0:
                    tacc = P.op("vector", lambda E, ts=ts, ncx=ncx, e=e, ps=ps:
                                E.tensor_scalar(
                                    out=ACC[:, ts * 2048 + ncx * 512:ts * 2048 + (ncx + 1) * 512],
                                    in0=ps[:, :], scalar1=gar[:, ts * NE + e:ts * NE + e + 1],
                                    scalar2=None, op0=ALU.mult),
                                waits=[tmm], sig="VE")
                else:
                    tacc = P.op("vector", lambda E, ts=ts, ncx=ncx, e=e, ps=ps:
                                E.scalar_tensor_tensor(
                                    out=ACC[:, ts * 2048 + ncx * 512:ts * 2048 + (ncx + 1) * 512],
                                    in0=ps[:, :], scalar=gar[:, ts * NE + e:ts * NE + e + 1],
                                    in1=ACC[:, ts * 2048 + ncx * 512:ts * 2048 + (ncx + 1) * 512],
                                    op0=ALU.mult, op1=ALU.add),
                                waits=[tmm, lastacc[key]], sig="VE")
                lastacc[key] = tacc
                pool.done(pi, tacc)
                pf = tacc
    tg3 = []
    for ts in range(8):
        tg3.append(P.op("scalar", lambda E, ts=ts: E.activation(
            out=ACC[:, ts * 2048:(ts + 1) * 2048],
            in_=ACC[:, ts * 2048:(ts + 1) * 2048], func=AF.Silu),
            waits=[lastacc[(ts, ncx)] for ncx in range(4)], sig="SC"))
    f1 = P.sbuf("f1", [64, DM], F32)
    f2 = P.sbuf("f2", [64, DM], F32)
    f3 = P.sbuf("f3", [64, DM], F32)
    woT = P.sbuf("woT", [64, DM], BF16)
    tw = cos_dev2(P, phoT, amoT, woT[:, :], f1, f2, f3)
    bsb = P.sbuf("bsb", [128, 16 * 64], BF16)
    tbl = [P.dma("sync", bsb[:, kt * 64:(kt + 1) * 64],
                 basoT[kt * 128:(kt + 1) * 128, :]) for kt in range(16)]
    OT = xnT
    ob = P.sbuf("ob", [128, DM], BF16)
    tp_tok = pf
    for ts in range(8):
        tin = P.dma("sync", ob[:, :], o3s[ts * 128:(ts + 1) * 128, :],
                    waits=[tp_tok] if tp_tok else ())
        tp_tok = emit_tp(P, lambda ri, cj: ob[:, cj * 128:(cj + 1) * 128],
                         lambda ri, cj, ts=ts: OT[:, cj * TPC + ts * 128:cj * TPC + ts * 128 + 128],
                         1, 16, P.ident, P.tps, waits=[tin])
    resoT = P.sbuf("resoT", [64, TPC], BF16)
    psr = P.psum("psr", [64, 512], F32)
    tres_l = []
    for tc in range(2):
        tmm = None
        for kt in range(16):
            tmm = P.op("tensor", lambda E, kt=kt, tc=tc: E.matmul(
                out=psr[:, :], lhsT=bsb[:, kt * 64:(kt + 1) * 64],
                rhs=OT[:, kt * TPC + tc * 512:kt * TPC + tc * 512 + 512],
                start=(kt == 0), stop=(kt == 15)),
                waits=[tp_tok] + tbl + tres_l if kt == 0 else (), sig="MM")
        tres_l.append(P.op("scalar", lambda E, tc=tc: ecopy(E, 
            out=resoT[0:64, tc * 512:(tc + 1) * 512], in_=psr[:, :]),
            waits=[tmm], sig="SC"))
    xst = StagePool(P, 1, [128, 512], "xs", F32)
    x1st = StagePool(P, 1, [128, 512], "x1s", F32)
    lastx = None
    for ts in range(8):
        for ncx in range(4):
            pi, ps, pb = pool.get()
            tmm = P.op("tensor", lambda E, ts=ts, ncx=ncx, ps=ps: E.matmul(
                out=ps[:, :], lhsT=resoT[0:64, ts * 128:ts * 128 + 128],
                rhs=woT[0:64, ncx * 512:(ncx + 1) * 512], start=True, stop=True,
                skip_group_check=True),
                waits=[tw] + tres_l + ([pb] if pb else []), sig="MM")
            xi, xs, xb = xst.get()
            txi = P.dma("sync", xs[:, :],
                        x[ts * 128:(ts + 1) * 128, ncx * 512:(ncx + 1) * 512],
                        waits=[xb] if xb else ())
            oi, os_, obb = x1st.get()
            t1 = P.op("vector", lambda E, ts=ts, ncx=ncx, ps=ps, os_=os_:
                      E.tensor_tensor(
                          out=os_[:, :], in0=ps[:, :],
                          in1=ACC[:, ts * 2048 + ncx * 512:ts * 2048 + (ncx + 1) * 512],
                          op=ALU.mult),
                      waits=[tmm, tg3[ts]] + ([obb] if obb else []), sig="VE")
            t2 = P.op("vector", lambda E, os_=os_, xs=xs: E.tensor_tensor(
                out=os_[:, :], in0=os_[:, :], in1=xs[:, :], op=ALU.add),
                waits=[t1, txi], sig="VE")
            pool.done(pi, t2)
            xst.done(xi, t2)
            lastx = P.dma("gpsimd",
                          X4[ts * 128:(ts + 1) * 128, ncx * 512:(ncx + 1) * 512],
                          os_[:, :], waits=[t2])
            x1st.done(oi, lastx)
    P.final_tokens.append(lastx)
    return P


def run_l1(prep, W, x1):
    P = build_l1tok()
    ims = []
    for c in range(8):
        b, q = c >> 2, c & 3
        ims.append(dict(
            x=np.ascontiguousarray(x1[b, q * 1024:(q + 1) * 1024]),
            WqT=W["WqT_bf"], WkT=W["WkT_bf"], WvT=W["WvT_bf"],
            rcos=np.ascontiguousarray(prep["rcos"][q * 1024:(q + 1) * 1024]),
            rsin=np.ascontiguousarray(prep["rsin"][q * 1024:(q + 1) * 1024]),
            ident=prep["ident"]))
    res = _launch(P, ims)
    QT = np.zeros((DM, NTOK), dtype=BF)
    KTg = np.zeros((DM, NTOK), dtype=BF)
    Vg = np.zeros((NTOK, DM), dtype=BF)
    for c in range(8):
        g = cslice(c)
        QT[:, g:g + 1024] = res.results[c]["QT"]
        KTg[:, g:g + 1024] = res.results[c]["KTo"]
        Vg[g:g + 1024, :] = res.results[c]["V"]
    P = build_l1attn()
    ims = []
    for c in range(8):
        pairs = [c * 4 + j for j in range(4)]
        qs_, ks_, vs_ = [], [], []
        for p in pairs:
            b, h = p >> 4, p & 15
            qs_.append(QT[h * 128:(h + 1) * 128, b * SEQ:(b + 1) * SEQ])
            ks_.append(KTg[h * 128:(h + 1) * 128, b * SEQ:(b + 1) * SEQ])
            vs_.append(Vg[b * SEQ:(b + 1) * SEQ, h * 128:(h + 1) * 128])
        ims.append(dict(qT=np.ascontiguousarray(np.concatenate(qs_, 0)),
                        kT=np.ascontiguousarray(np.concatenate(ks_, 0)),
                        vp=np.ascontiguousarray(np.stack(vs_, 0)),
                        masks=prep["masks"]))
    res = _launch(P, ims)
    O1 = np.zeros((NTOK, DM), dtype=BF)
    for c in range(8):
        for j in range(4):
            p = c * 4 + j
            b, h = p >> 4, p & 15
            O1[b * SEQ:(b + 1) * SEQ, h * 128:(h + 1) * 128] = res.results[c]["o3" if False else "o1"][j]
    P = build_l1ffn()
    ims = []
    for c in range(8):
        b, q = c >> 2, c & 3
        g = cslice(c)
        ims.append(dict(
            x=np.ascontiguousarray(x1[b, q * 1024:(q + 1) * 1024]),
            o1s=np.ascontiguousarray(O1[g:g + 1024]),
            WoT=W["WoT_bf"], gateT=W["gateT_bf"], upT=W["upT_bf"],
            downT=W["downT_bf"], ident=prep["ident"]))
    res = _launch(P, ims)
    x2 = np.zeros((BATCH, SEQ, DM), np.float32)
    for c in range(8):
        b, q = c >> 2, c & 3
        x2[b, q * 1024:(q + 1) * 1024] = res.results[c]["X2"]
    return x2


def run_l2(prep, W, x2):
    P = build_l2tok()
    ims = []
    for c in range(8):
        b, q = c >> 2, c & 3
        t0 = q * 1024
        xe = np.zeros((1152, DM), np.float32)
        xe[1:1025] = x2[b, t0:t0 + 1024]
        if t0 > 0:
            xe[0] = x2[b, t0 - 1]
        ims.append(dict(xe=xe, W2kT=W["W2kT_bf"], W2vT=W["W2vT_bf"],
                        W2rT=W["W2rT_bf"], ident=prep["ident"]))
    res = _launch(P, ims)
    ATS = np.zeros((2, DM, SEQ), np.float32)
    EKS = np.zeros((2, DM, SEQ), np.float32)
    RTS = np.zeros((2, DM, SEQ), np.float32)
    for c in range(8):
        b, q = c >> 2, c & 3
        sl = slice(q * 1024, (q + 1) * 1024)
        ATS[b, :, sl] = res.results[c]["ATS"]
        EKS[b, :, sl] = res.results[c]["EKS"]
        RTS[b, :, sl] = res.results[c]["RTS"]
    P = build_l2scan()
    ims = []
    for c in range(8):
        b, cr = c >> 2, (c & 3) * 512
        ims.append(dict(ats=np.ascontiguousarray(ATS[b, cr:cr + 512]),
                        eks=np.ascontiguousarray(EKS[b, cr:cr + 512]),
                        rts=np.ascontiguousarray(RTS[b, cr:cr + 512]),
                        td_c=np.ascontiguousarray(
                            prep["td"][cr:cr + 512].reshape(4, 128).T),
                        ident=prep["ident"]))
    res = _launch(P, ims)
    S2 = np.zeros((2, SEQ, DM), dtype=BF)
    for c in range(8):
        b, cr = c >> 2, (c & 3) * 512
        S2[b, :, cr:cr + 512] = res.results[c]["S2K"]
    P = build_l2ffn()
    ims = []
    for c in range(8):
        b, q = c >> 2, c & 3
        ims.append(dict(
            x=np.ascontiguousarray(x2[b, q * 1024:(q + 1) * 1024]),
            s2s=np.ascontiguousarray(S2[b, q * 1024:(q + 1) * 1024]),
            gn=prep["gn2"], dn2=prep["dn2"], QG2b=W["QG2b"], QD2b=W["QD2b"],
            wmeans=W["wmeans"], ident=prep["ident"]))
    res = _launch(P, ims)
    x3 = np.zeros((BATCH, SEQ, DM), np.float32)
    for c in range(8):
        b, q = c >> 2, c & 3
        x3[b, q * 1024:(q + 1) * 1024] = res.results[c]["X3"]
    return x3


def run_l3(prep, W, x3):
    P = build_l3tok()
    ims = []
    for c in range(8):
        b, q = c >> 2, c & 3
        m = dict(x=np.ascontiguousarray(x3[b, q * 1024:(q + 1) * 1024]),
                 ident=prep["ident"])
        for nm in ("q", "k", "v"):
            m[f"bas{nm}T"] = W[f"bas{nm}T_bf"]
            m[f"ph{nm}T"] = prep[f"ph{nm}T"]
            m[f"am{nm}T"] = prep[f"amp{nm}T"]
        ims.append(m)
    res = _launch(P, ims)
    QT = np.zeros((DM, NTOK), dtype=BF)
    KTg = np.zeros((DM, NTOK), dtype=BF)
    Vg = np.zeros((NTOK, DM), dtype=BF)
    for c in range(8):
        g = cslice(c)
        QT[:, g:g + 1024] = res.results[c]["QT"]
        KTg[:, g:g + 1024] = res.results[c]["KTo"]
        Vg[g:g + 1024, :] = res.results[c]["V"]
    P = build_l3attn()
    ims = []
    for c in range(8):
        pairs = [c * 4 + j for j in range(4)]
        qs_, ks_, vs_ = [], [], []
        for p in pairs:
            b, h = p >> 4, p & 15
            qs_.append(QT[h * 128:(h + 1) * 128, b * SEQ:(b + 1) * SEQ])
            ks_.append(KTg[h * 128:(h + 1) * 128, b * SEQ:(b + 1) * SEQ])
            vs_.append(Vg[b * SEQ:(b + 1) * SEQ, h * 128:(h + 1) * 128])
        ims.append(dict(qT=np.ascontiguousarray(np.concatenate(qs_, 0)),
                        kT=np.ascontiguousarray(np.concatenate(ks_, 0)),
                        vp=np.ascontiguousarray(np.stack(vs_, 0)),
                        ident=prep["ident"]))
    res = _launch(P, ims)
    O3 = np.zeros((NTOK, DM), dtype=BF)
    for c in range(8):
        for j in range(4):
            p = c * 4 + j
            b, h = p >> 4, p & 15
            O3[b * SEQ:(b + 1) * SEQ, h * 128:(h + 1) * 128] = res.results[c]["o3"][j]
    P = build_l3ffn()
    ims = []
    for c in range(8):
        b, q = c >> 2, c & 3
        g = cslice(c)
        ims.append(dict(
            x=np.ascontiguousarray(x3[b, q * 1024:(q + 1) * 1024]),
            o3s=np.ascontiguousarray(O3[g:g + 1024]),
            basoT=W["basoT_bf"], phoT=prep["phoT"], amoT=prep["ampoT"],
            Wr3b=W["Wr3_bf"], We3rb=W["We3r_bf"], be3b=W["be3row_bf"],
            ones_r=prep["ones_r"], ident=prep["ident"]))
    res = _launch(P, ims)
    x4 = np.zeros((BATCH, SEQ, DM), np.float32)
    for c in range(8):
        b, q = c >> 2, c & 3
        x4[b, q * 1024:(q + 1) * 1024] = res.results[c]["X4"]
    return x4
